# version 16
# baseline (speedup 1.0000x reference)
import os
import numpy as np
import ml_dtypes
from contextlib import ExitStack

import concourse.bass as bass
import concourse.mybir as mybir
from concourse.bass_utils import run_bass_kernel_spmd

F32 = mybir.dt.float32
BF16 = mybir.dt.bfloat16
AF = mybir.ActivationFunctionType
ALU = mybir.AluOpType
AX = mybir.AxisListType

NH, DKH, D, SEQ = 16, 64, 1024, 4096
NQT, QTW = 4, 512
OWN = NQT * QTW
NCK = [8, 16, 24, 32]
TILESETS = [[0, 3, 4, 7], [1, 2, 5, 6]]
ISQ = float(8.0 ** -0.5)
NE, DI = 8, 2048
EPS = 1e-5
_NOFILL = os.environ.get("K_NOFILL", "0") == "1"
_NOTR2 = True


def ts_(i, w):
    return slice(i * w, (i + 1) * w)


class Walker:

    def __init__(self, nc, stack):
        self.nc = nc
        self.stack = stack
        self.thunks = {"sync": [], "gpsimd": [], "vector": [], "scalar": [], "tensor": []}
        self.sems = {}
        self.cnt = {}
        self.bank_gate = {}

    def sem(self, name):
        self.sems[name] = self.stack.enter_context(self.nc.semaphore(name))
        self.cnt[name] = 0

    def wait(self, eng, sem, val):
        if val is None or val <= 0:
            return
        h = self.sems[sem]
        self.thunks[eng].append(lambda e, h=h, v=val: e.wait_ge(h, v))

    def run(self, eng, f, sem=None, k=1):
        h = self.sems[sem] if sem else None

        def thunk(e, f=f, h=h, k=k):
            inst = f(e)
            if h is not None:
                inst.then_inc(h, k)

        self.thunks[eng].append(thunk)
        if sem is not None:
            self.cnt[sem] += k
            return self.cnt[sem]
        return None

    def use_bank(self, bid, eng="tensor"):
        g = self.bank_gate.get(bid)
        if g:
            self.wait(eng, g[0], g[1])

    def release_bank(self, bid, sem, cnt):
        self.bank_gate[bid] = (sem, cnt)


def build_nc(debug=False, enddbg=False, phases="abc", c_halves=2, c_nexp=NE, n_tc=8, wprep=True, aparts="lntkvm"):
    nc = bass.Bass("TRN2", target_bir_lowering=False, debug=False,
                   detect_race_conditions=False)

    x_seq = nc.dram_tensor("x_seq", [SEQ, D], F32, kind="ExternalInput").ap()
    x_own = nc.dram_tensor("x_own", [OWN, D], F32, kind="ExternalInput").ap()
    m_in = nc.dram_tensor("m_in", [NQT, 8, 128, QTW], BF16, kind="ExternalInput").ap()
    wq_in = nc.dram_tensor("wq_in", [D, D], F32, kind="ExternalInput").ap()
    wk_in = nc.dram_tensor("wk_in", [D, D], F32, kind="ExternalInput").ap()
    wv_in = nc.dram_tensor("wv_in", [D, D], F32, kind="ExternalInput").ap()
    wo_in = nc.dram_tensor("wo_in", [D, D], F32, kind="ExternalInput").ap()
    bq_in = nc.dram_tensor("bq_in", [D], F32, kind="ExternalInput").ap()
    bk_in = nc.dram_tensor("bk_in", [D], F32, kind="ExternalInput").ap()
    bv_in = nc.dram_tensor("bv_in", [D], F32, kind="ExternalInput").ap()
    bo_in = nc.dram_tensor("bo_in", [D], F32, kind="ExternalInput").ap()
    rw_in = nc.dram_tensor("rw_in", [NE, D], F32, kind="ExternalInput").ap()
    wg_in = nc.dram_tensor("wg_in", [NE, D, DI], F32, kind="ExternalInput").ap()
    wu_in = nc.dram_tensor("wu_in", [NE, D, DI], F32, kind="ExternalInput").ap()
    wd_in = nc.dram_tensor("wd_in", [NE, DI, D], F32, kind="ExternalInput").ap()

    out = nc.dram_tensor("out", [OWN, D], F32, kind="ExternalOutput").ap()
    x1_dram = nc.dram_tensor("x1_scr", [OWN, D], F32, kind="Internal").ap()

    dbg = {}
    if enddbg:
        dbg["xh2f0"] = nc.dram_tensor("dbg_xh2f0", [8, 128, 1024], F32, kind="ExternalOutput").ap()
        dbg["aff0"] = nc.dram_tensor("dbg_aff0", [128, 8, 8], F32, kind="ExternalOutput").ap()
    if debug:
        dbg["k"] = nc.dram_tensor("dbg_k", [8, 128, SEQ], F32, kind="ExternalOutput").ap()
        dbg["v"] = nc.dram_tensor("dbg_v", [32, 128, NH * 65], F32, kind="ExternalOutput").ap()
        dbg["km2"] = nc.dram_tensor("dbg_km2", [16, 1], F32, kind="ExternalOutput").ap()
        dbg["x1"] = nc.dram_tensor("dbg_x1", [OWN, D], F32, kind="ExternalOutput").ap()
        dbg["aff"] = nc.dram_tensor("dbg_aff", [2, 128, 8, 8], F32, kind="ExternalOutput").ap()
        dbg["act"] = nc.dram_tensor("dbg_act", [16, 128, 1024], F32, kind="ExternalOutput").ap()
        dbg["xhf"] = nc.dram_tensor("dbg_xhf", [8, 128, 512], F32, kind="ExternalOutput").ap()
        dbg["xh"] = nc.dram_tensor("dbg_xh", [128, 2, 1024], F32, kind="ExternalOutput").ap()
        dbg["xh2f"] = nc.dram_tensor("dbg_xh2f", [8, 128, 1024], F32, kind="ExternalOutput").ap()

    stack = ExitStack()
    nc._walker_stack = stack

    w = Walker(nc, stack)
    for s in ("ld", "tr", "tr2", "wd", "st", "dst", "tc", "vc", "sc", "gc"):
        w.sem(s)

    en = stack.enter_context

    xh = en(nc.sbuf_tensor("xh", [128, 2, D], F32))
    ones_all = en(nc.sbuf_tensor("ones_all", [16, 128], BF16))
    id16 = en(nc.sbuf_tensor("id16", [16, 16], BF16))
    id128f = en(nc.sbuf_tensor("id128f", [128, 128], F32))
    bqk_c = en(nc.sbuf_tensor("bqk_c", [128, 2, 8], F32))
    bn6 = en(nc.sbuf_tensor("bn6", [128, 12], F32))
    st4 = en(nc.sbuf_tensor("st4", [128, 4], F32))
    lnt = en(nc.sbuf_tensor("lnt", [128, 8], F32))
    nrmL = en(nc.sbuf_tensor("nrmL", [128, 8, 16], BF16))
    psA = en(nc.psum_tensor("psA", [128, 2, 512], F32))
    psB = en(nc.psum_tensor("psB", [128, 2, 512], F32))
    psD = en(nc.psum_tensor("psD", [128, 2, 512], F32))
    psC = en(nc.psum_tensor("psC", [128, 2, 512], F32))
    pairs = [psA[:], psB[:], psC[:]]
    ps = [psA[:, 0, :], psA[:, 1, :], psB[:, 0, :], psB[:, 1, :],
          psD[:, 0, :], psD[:, 1, :], psC[:, 0, :], psC[:, 1, :]]

    stackAB = ExitStack()
    enAB = stackAB.enter_context
    xo = enAB(nc.sbuf_tensor("xo", [128, 1, D], F32))
    idt128 = xo[:, 0, 0:128]
    xhf = enAB(nc.sbuf_tensor("xhf", [128, 8, QTW], BF16))
    ktens = enAB(nc.sbuf_tensor("ktens", [128, 8, SEQ], BF16))
    vtens = enAB(nc.sbuf_tensor("vtens", [128, 32, NH * 65], BF16))
    km2 = enAB(nc.sbuf_tensor("km2", [16, 1], F32))
    cm2 = enAB(nc.sbuf_tensor("cm2", [16, 1], F32))
    ksq = enAB(nc.sbuf_tensor("ksq", [128, 2, 512], BF16))
    idtmp = ksq[0:16, 1, 0:16]
    cmdiag = ksq[0:16, 0, 0:16]
    vpad = vtens[:].rearrange("p c (h e) -> p c h e", e=65)

    for j, b in enumerate((bq_in, bk_in)):
        src = b.rearrange("(oc p) -> p oc", p=128)
        w.run("sync", lambda e, j=j, src=src: e.dma_start(out=bqk_c[:, j, :], in_=src,
                                                          allow_slow_non_contiguous=True), "ld", 16)
    bld = w.cnt["ld"]
    w.run("gpsimd", lambda e: e.iota(idtmp, pattern=[[1, 16]], base=0, channel_multiplier=-1,
                                     allow_small_or_imprecise_dtypes=True), "gc")
    w.run("gpsimd", lambda e: e.iota(idt128, pattern=[[1, 128]], base=0, channel_multiplier=-1,
                                     allow_small_or_imprecise_dtypes=True), "gc")
    iota_done = w.cnt["gc"]

    w.run("vector", lambda e: e.memset(ones_all[:], 1.0), "vc")
    w.run("vector", lambda e: e.memset(km2[:], 0.0), "vc")
    w.wait("vector", "ld", bld)
    for j in range(2):
        w.run("vector", lambda e, j=j: e.tensor_scalar(out=bqk_c[:, j, :], in0=bqk_c[:, j, :],
                                                       scalar1=ISQ, scalar2=None, op0=ALU.mult), "vc")
    w.wait("vector", "gc", iota_done)
    w.run("vector", lambda e: e.tensor_single_scalar(out=id16[:], in_=idtmp, scalar=0.0,
                                                     op=ALU.is_equal), "vc")
    w.run("vector", lambda e: e.tensor_single_scalar(out=id128f[:], in_=idt128, scalar=0.0,
                                                     op=ALU.is_equal), "vc")
    w.run("vector", lambda e: e.memset(nrmL[:], 0.0), "vc")
    for oc in range(8):
        w.run("vector", lambda e, oc=oc: e.memset(nrmL[0:64, oc, 2 * oc: 2 * oc + 1], 1.0), "vc")
        w.run("vector", lambda e, oc=oc: e.memset(nrmL[64:128, oc, 2 * oc + 1: 2 * oc + 2], 1.0), "vc")
    setup_v = w.cnt["vc"]


    def emit_ln(src_ap, dst_ap, n_ts, src_off=0, dst_off=0):
        for t in range(n_ts):
            si, di = src_off + t, dst_off + t
            w.run("vector", lambda e, si=si: e.bn_stats(out=bn6[:, 0:6], in_=src_ap[:, si, 0:512]), "vc")
            w.run("vector", lambda e, si=si: e.bn_stats(out=bn6[:, 6:12], in_=src_ap[:, si, 512:1024]), "vc")
            w.run("vector", lambda e: e.bn_aggr(out=st4[:, 0:2], in_=bn6[:, 0:6]), "vc")
            w.run("vector", lambda e: e.bn_aggr(out=st4[:, 2:4], in_=bn6[:, 6:12]), "vc")
            w.run("vector", lambda e: e.drain(), None)
            w.run("vector", lambda e: e.tensor_scalar(out=lnt[:, 1:2], in0=st4[:, 0:1],
                                                      scalar1=st4[:, 2:3], scalar2=0.5,
                                                      op0=ALU.add, op1=ALU.mult), "vc")
            w.run("vector", lambda e: e.tensor_sub(out=lnt[:, 2:3], in0=st4[:, 0:1], in1=st4[:, 2:3]), "vc")
            w.run("vector", lambda e: e.tensor_scalar(out=lnt[:, 5:6], in0=st4[:, 1:2],
                                                      scalar1=st4[:, 3:4], scalar2=0.5,
                                                      op0=ALU.add, op1=ALU.mult), "vc")
            w.run("vector", lambda e: e.drain(), None)
            w.run("vector", lambda e: e.tensor_scalar(out=lnt[:, 3:4], in0=lnt[:, 2:3],
                                                      scalar1=lnt[:, 2:3], scalar2=4.0 * EPS,
                                                      op0=ALU.mult, op1=ALU.add), "vc")
            w.run("vector", lambda e: e.drain(), None)
            vvar = w.run("vector", lambda e: e.tensor_scalar(out=lnt[:, 6:7], in0=lnt[:, 3:4], scalar1=0.25,
                                                             scalar2=lnt[:, 5:6], op0=ALU.mult, op1=ALU.add), "vc")
            w.wait("scalar", "vc", vvar)
            w.run("scalar", lambda e: e.activation(out=lnt[:, 7:8], in_=lnt[:, 6:7], func=AF.Ln,
                                                   bias=0.0, scale=1.0), "sc")
            w.run("scalar", lambda e: e.drain(), None)
            sr = w.run("scalar", lambda e: e.activation(out=lnt[:, 6:7], in_=lnt[:, 7:8], func=AF.Exp,
                                                        bias=0.0, scale=-0.5), "sc")
            w.wait("vector", "sc", sr)
            w.run("vector", lambda e, si=si, di=di: e.tensor_scalar(
                out=dst_ap[:, di, :], in0=src_ap[:, si, :],
                scalar1=lnt[:, 1:2], scalar2=lnt[:, 6:7],
                op0=ALU.subtract, op1=ALU.mult), "vc")
        return w.cnt["vc"]

    def emit_transposes(src_ap, dst_ap, n_ts, src_off=0, dst_toff=0, banks=(5, 6, 7)):
        for hc in range(8):
            bid = banks[hc % len(banks)]
            bank = ps[bid]
            w.use_bank(bid)
            for t in range(n_ts):
                mmt = w.run("tensor", lambda e, t=t, hc=hc, bank=bank: e.transpose(
                    bank[:, ts_(t, 128)], src_ap[:, t, ts_(hc, 128)], id128f[:]),
                    "tc" if t == n_ts - 1 else None)
            w.wait("scalar", "tc", mmt)
            dst = dst_ap[:, hc, dst_toff + src_off * 128: dst_toff + (src_off + n_ts) * 128]
            w.run("scalar", lambda e, dst=dst, bank=bank, n_ts=n_ts: e.copy(
                out=dst, in_=bank[:, 0: n_ts * 128]), None)
            ev = w.run("scalar", lambda e: e.drain(), "tr")
            w.release_bank(bid, "tr", ev)
        return (w.cnt["tr"], w.cnt["tr2"])

    def emit_weight_prep(w_hbm, wtmp, wT):
        for g in range(4):
            src = w_hbm[ts_(g, 256), :].rearrange("(oc p) h -> p oc h", p=128)
            w.wait("sync", "tr", w.cnt["tr"])
            dcnt = w.run("sync", lambda e, src=src: e.dma_start(out=wtmp[:], in_=src), "ld", 16)
            w.wait("tensor", "ld", dcnt)
            for hc in range(8):
                bid = (6, 7)[hc % 2]
                bank = ps[bid]
                w.use_bank(bid)
                for oc in range(2):
                    mmt = w.run("tensor", lambda e, hc=hc, oc=oc, bank=bank: e.transpose(
                        bank[:, ts_(oc, 128)], wtmp[:, oc, ts_(hc, 128)], id128f[:]),
                        "tc" if oc == 1 else None)
                w.wait("scalar", "tc", mmt)
                dst = wT[:, hc, g * 256: (g + 1) * 256]
                ev = w.run("scalar", lambda e, dst=dst, bank=bank: e.copy(
                    out=dst, in_=bank[:, 0:256]), "tr")
                w.release_bank(bid, "tr", ev)
        return (w.cnt["tr"], w.cnt["tr2"])

    gate = {"xo_vc": setup_v, "xh_tr": (0, 0), "xhf_tc": 0, "xo_st": 0}

    def emit_x_tile(rows_ap, dst=None, xhf_gate_tc=None):
        dst = xhf if dst is None else dst
        gtc = gate["xhf_tc"] if xhf_gate_tc is None else xhf_gate_tc
        for u in range(4):
            src = rows_ap[ts_(u, 128), :].rearrange("(t p) h -> p t h", p=128)
            w.wait("sync", "vc", gate["xo_vc"])
            w.wait("sync", "st", gate["xo_st"])
            ldx = w.run("sync", lambda e, src=src: e.dma_start(out=xo[:], in_=src), "ld", 16)
            w.wait("vector", "ld", ldx)
            if u % 2 == 0:
                w.wait("vector", "tr", gate["xh_tr"][0])
            ln_done = emit_ln(xo[:], xh[:], 1, dst_off=u % 2)
            gate["xo_vc"] = ln_done
            if u == 1:
                w.wait("tensor", "tc", gtc)
            if u % 2 == 1:
                w.wait("tensor", "vc", ln_done)
                gate["xh_tr"] = emit_transposes(xh[:], dst, 2, src_off=0,
                                                dst_toff=(u // 2) * 256)
        return gate["xh_tr"]

    stackA = ExitStack()
    enA = stackA.enter_context
    wkT = enA(nc.sbuf_tensor("wkT", [128, 8, D], BF16))
    wvT = enA(nc.sbuf_tensor("wvT", [128, 8, D], BF16))
    wtmpA = enA(nc.sbuf_tensor("wtmpA", [128, 2, D], F32))
    bv_row = enA(nc.sbuf_tensor("bv_row", [1, D], BF16))
    xhf2 = enA(nc.sbuf_tensor("xhf2", [128, 8, QTW], BF16))

    w.run("gpsimd", lambda e: e.dma_start(out=bv_row[:], in_=bv_in.rearrange("(o h) -> o h", o=1)), "wd", 16)
    if wprep:
        emit_weight_prep(wk_in, wtmpA, wkT)
        trv, trv2 = emit_weight_prep(wv_in, wtmpA, wvT)
    else:
        trv, trv2 = w.cnt["tr"], w.cnt["tr2"]

    vones = w.run("vector", lambda e: e.memset(vpad[:, :, :, 64:65], 1.0), "vc")

    w.wait("tensor", "tr", trv)
    w.wait("tensor", "tr2", trv2)
    w.wait("tensor", "vc", setup_v)
    w.wait("scalar", "vc", setup_v)
    wd_bias = w.cnt["wd"]

    xhf_mm_done = {0: 0, 1: 0}
    for tc in range(n_tc):
        buf = xhf if tc % 2 == 0 else xhf2
        if "l" in aparts:
            emit_x_tile(x_seq[ts_(tc, 512), :], dst=buf, xhf_gate_tc=xhf_mm_done[tc % 2])
        w.wait("tensor", "tr", gate["xh_tr"][0])
        w.wait("tensor", "tr2", gate["xh_tr"][1])

        kev_sc = {}
        if "k" not in aparts:
            continue
        for oc in range(8):
            bid = oc % 2
            bank = ps[bid]
            w.use_bank(bid)
            for hc in range(8):
                mmk = w.run("tensor", lambda e, hc=hc, oc=oc, bank=bank, buf=buf: e.matmul(
                    bank, wkT[:, hc, ts_(oc, 128)], buf[:, hc, :],
                    start=(hc == 0), stop=(hc == 7)), "tc" if hc == 7 else None)
            w.wait("scalar", "tc", mmk)
            kev_sc[oc] = w.run("scalar", lambda e, tc=tc, oc=oc, bank=bank: e.activation(
                out=ktens[:, oc, ts_(tc, 512)], in_=bank, func=AF.Identity,
                bias=bqk_c[:, 1, oc: oc + 1], scale=ISQ), "sc")
            w.release_bank(bid, "sc", kev_sc[oc])

        if "v" not in aparts:
            gate["xhf_tc"] = w.cnt["tc"]
            continue
        w.wait("tensor", "vc", vones)
        for t in range(4):
            for oh in range(2):
                bid = 2 + (t * 2 + oh) % 2
                bank = ps[bid]
                w.use_bank(bid)
                for hc in range(8):
                    w.run("tensor", lambda e, hc=hc, t=t, oh=oh, bank=bank, buf=buf: e.matmul(
                        bank, buf[:, hc, ts_(t, 128)], wvT[:, hc, ts_(oh, 512)],
                        start=(hc == 0), stop=False), None)
                mmv = w.run("tensor", lambda e, oh=oh, bank=bank: e.matmul(
                    bank, ones_all[0:1, 0:128], bv_row[0:1, ts_(oh, 512)],
                    start=False, stop=True), "tc")
                w.wait("vector", "tc", mmv)
                kc = tc * 4 + t
                dst = vpad[:, kc, ts_(oh, 8), 0:64]
                srcp = bank.rearrange("p (h d) -> p h d", d=64)
                vcnt = w.run("vector", lambda e, dst=dst, srcp=srcp: e.tensor_copy(
                    out=dst, in_=srcp), "vc")
                w.release_bank(bid, "vc", vcnt)
        xhf_mm_done[tc % 2] = w.cnt["tc"]
        gate["xhf_tc"] = w.cnt["tc"]

        if "m" not in aparts:
            continue
        w.use_bank(4)
        nrm_mm = {}
        for oc in range(8):
            sq = ksq[:, oc % 2, :]
            w.wait("vector", "sc", kev_sc[oc])
            if oc >= 2:
                w.wait("vector", "tc", nrm_mm[oc - 2])
            vsq = w.run("vector", lambda e, sq=sq, oc=oc, tc=tc: e.tensor_mul(
                out=sq, in0=ktens[:, oc, ts_(tc, 512)], in1=ktens[:, oc, ts_(tc, 512)]), "vc")
            w.wait("tensor", "vc", vsq)
            nrm_mm[oc] = w.run("tensor", lambda e, sq=sq, oc=oc: e.matmul(
                ps[4][0:16, :], nrmL[:, oc, :], sq,
                start=(oc == 0), stop=(oc == 7)), "tc")
        w.wait("vector", "tc", nrm_mm[7])
        w.run("vector", lambda e: e.reduce_max(out=lnt[0:16, 0:1], in_=ps[4][0:16, :], axis=AX.X), "vc")
        w.run("vector", lambda e: e.drain(), None)
        nm = w.run("vector", lambda e: e.tensor_max(out=km2[:], in0=km2[:], in1=lnt[0:16, 0:1]), "vc")
        w.release_bank(4, "vc", nm)

    phaseA_vc, phaseA_sc = w.cnt["vc"], w.cnt["sc"]

    if debug:
        w.wait("gpsimd", "sc", phaseA_sc)
        w.wait("gpsimd", "vc", phaseA_vc)
        w.wait("gpsimd", "tr", w.cnt["tr"])
        for hc in range(8):
            w.run("gpsimd", lambda e, hc=hc: e.dma_start(out=dbg["xhf"][hc], in_=xhf[:, hc, :]), "dst", 16)
        w.run("gpsimd", lambda e: e.dma_start(out=dbg["xh"], in_=xh[:]), "dst", 16)
        for oc in range(8):
            w.run("gpsimd", lambda e, oc=oc: e.dma_start(out=dbg["k"][oc], in_=ktens[:, oc, :]), "dst", 16)
        for kc in range(32):
            w.run("gpsimd", lambda e, kc=kc: e.dma_start(out=dbg["v"][kc], in_=vtens[:, kc, :]), "dst", 16)
        w.run("gpsimd", lambda e: e.dma_start(out=dbg["km2"], in_=km2[:]), "dst", 16)

    if "b" not in phases:
        if debug:
            w.wait("sync", "st", w.cnt["st"])
            w.wait("gpsimd", "dst", w.cnt["dst"])
        return _replay(nc, w)
    stackA.close()
    stackB = ExitStack()
    enB = stackB.enter_context
    stackB2 = ExitStack()
    wqT = enB(nc.sbuf_tensor("wqT", [128, 8, D], BF16))
    woT = enB(nc.sbuf_tensor("woT", [128, 8, D], BF16))
    bo_row = enB(nc.sbuf_tensor("bo_row", [1, D], BF16))
    wtmpB = stackB2.enter_context(nc.sbuf_tensor("wtmpB", [128, 2, D], F32))

    w.wait("gpsimd", "tc", gate["xhf_tc"])
    w.wait("sync", "tc", gate["xhf_tc"])
    w.run("gpsimd", lambda e: e.dma_start(out=bo_row[:], in_=bo_in.rearrange("(o h) -> o h", o=1)), "wd", 16)
    emit_weight_prep(wq_in, wtmpB, wqT)
    trq, trq2 = emit_weight_prep(wo_in, wtmpB, woT)
    stackB2.close()

    qa = enB(nc.sbuf_tensor("qa", [128, 8, QTW], BF16))
    ptring = enB(nc.sbuf_tensor("ptring", [128, 4, QTW], BF16))
    praw = xhf[0:1, 7, :]
    lbc_sb = enB(nc.sbuf_tensor("lbc_sb", [64, QTW], BF16))
    cnegb = enB(nc.sbuf_tensor("cnegb", [128, 16], F32))
    mslot = enB(nc.sbuf_tensor("mslot", [128, 8, QTW], BF16))

    w.wait("tensor", "tr", trq)
    w.wait("tensor", "tr2", trq2)
    mask_use_vc = 0
    tile_av_end = 0
    phaseA_end_tc = gate["xhf_tc"]

    for i in range(NQT):
        nck = NCK[i]
        if i == 0:
            w.wait("sync", "tc", phaseA_end_tc)
        w.wait("sync", "vc", mask_use_vc)
        msrc = m_in[i].rearrange("c p j -> p c j")
        mld = w.run("sync", lambda e, msrc=msrc: e.dma_start(out=mslot[:], in_=msrc), "ld", 16)

        emit_x_tile(x_own[ts_(i, QTW), :])
        w.wait("tensor", "tr", gate["xh_tr"][0])
        w.wait("tensor", "tr2", gate["xh_tr"][1])

        qev_sc = {}
        for oc in range(8):
            bid = oc % 2
            bank = ps[bid]
            w.use_bank(bid)
            for hc in range(8):
                mmq = w.run("tensor", lambda e, hc=hc, oc=oc, bank=bank: e.matmul(
                    bank, wqT[:, hc, ts_(oc, 128)], xhf[:, hc, :],
                    start=(hc == 0), stop=(hc == 7)), "tc" if hc == 7 else None)
            w.wait("scalar", "tc", mmq)
            qev_sc[oc] = w.run("scalar", lambda e, oc=oc, bank=bank: e.activation(
                out=qa[:, oc, :], in_=bank, func=AF.Identity,
                bias=bqk_c[:, 0, oc: oc + 1], scale=ISQ), "sc")
            w.release_bank(bid, "sc", qev_sc[oc])
        gate["xhf_tc"] = w.cnt["tc"]

        w.use_bank(4)
        nrm_mm = {}
        for oc in range(8):
            sq = ksq[:, oc % 2, :]
            w.wait("vector", "sc", qev_sc[oc])
            if oc < 2:
                w.wait("vector", "tc", tile_av_end)
            if oc >= 2:
                w.wait("vector", "tc", nrm_mm[oc - 2])
            vsq = w.run("vector", lambda e, sq=sq, oc=oc: e.tensor_mul(
                out=sq, in0=qa[:, oc, :], in1=qa[:, oc, :]), "vc")
            w.wait("tensor", "vc", vsq)
            nrm_mm[oc] = w.run("tensor", lambda e, sq=sq, oc=oc: e.matmul(
                ps[4][0:16, :], nrmL[:, oc, :], sq,
                start=(oc == 0), stop=(oc == 7)), "tc")
        w.wait("vector", "tc", nrm_mm[7])
        w.run("vector", lambda e: e.reduce_max(out=lnt[0:16, 0:1], in_=ps[4][0:16, :], axis=AX.X), "vc")
        w.run("vector", lambda e: e.drain(), None)
        vq = w.run("vector", lambda e: e.tensor_max(out=cm2[:], in0=km2[:], in1=lnt[0:16, 0:1]), "vc")
        w.release_bank(4, "vc", vq)
        w.run("vector", lambda e: e.drain(), None)
        vcd = w.run("vector", lambda e: e.tensor_scalar(out=cmdiag, in0=id16[:],
                                                        scalar1=cm2[0:16, 0:1], scalar2=-1.0,
                                                        op0=ALU.mult, op1=ALU.mult), "vc")
        w.use_bank(5)
        w.wait("tensor", "vc", vcd)
        mmc = w.run("tensor", lambda e: e.matmul(ps[5][:, 0:16], ones_all[0:16, 0:128], cmdiag,
                                                 start=True, stop=True), "tc")
        w.wait("scalar", "tc", mmc)
        scb = w.run("scalar", lambda e: e.copy(out=cnegb[:], in_=ps[5][:, 0:16]), "sc")
        w.release_bank(5, "sc", scb)
        w.run("scalar", lambda e: e.drain(), None)

        w.wait("scalar", "ld", mld)
        w.wait("vector", "ld", mld)
        w.wait("scalar", "sc", scb)
        prev_lbc_mm = 0
        last_mask_vc = 0
        for hd in range(NH):
            oc, p = hd // 2, hd % 2
            ov_bid = 4 + hd % 2
            ov = ps[ov_bid]
            w.use_bank(ov_bid)
            w.wait("tensor", "sc", qev_sc[oc])
            ngrp = nck // 2
            st_mm = {}
            pt_prod = {}
            av_g = {}
            pr_bank = {0: (0, 1), 1: (2, 3), 2: (6, 7)}
            pt_slots = {0: ptring[:, 0:2, :], 1: ptring[:, 2:4, :], 2: ksq[:, 0:2, :]}

            def emit_st(g):
                pr = g % 3
                b0, b1 = pr_bank[pr]
                w.use_bank(b0)
                w.use_bank(b1)
                for j in (0, 1):
                    c = 2 * g + j
                    st_mm[g] = w.run("tensor", lambda e, c=c, oc=oc, p=p, b=pr_bank[pr][j]: e.matmul(
                        ps[b], ktens[64 * p:64 * (p + 1), oc, ts_(c, 128)],
                        qa[64 * p:64 * (p + 1), oc, :],
                        start=True, stop=True, tile_position=(64 * p, 0)),
                        "tc" if j == 1 else None)

            def emit_exp(g):
                nonlocal last_mask_vc
                pr = g % 3
                slots = pt_slots[pr]
                masked = 2 * g >= nck - 8
                w.wait("scalar", "tc", st_mm[g])
                if av_g.get(g - 3):
                    w.wait("scalar", "tc", av_g[g - 3])
                se = w.run("scalar", lambda e, pr=pr, hd=hd, slots=slots: e.activation(
                    out=slots, in_=pairs[pr], func=AF.Exp,
                    bias=cnegb[:, hd: hd + 1], scale=1.0), "sc")
                if masked:
                    w.wait("vector", "sc", se)
                    for j in (0, 1):
                        c = 2 * g + j
                        last_mask_vc = w.run("vector", lambda e, c=c, j=j, slots=slots, nck=nck: e.tensor_mul(
                            out=slots[:, j, :], in0=slots[:, j, :],
                            in1=mslot[:, c - (nck - 8), :]), "vc")
                    pt_prod[g] = ("vc", last_mask_vc)
                else:
                    pt_prod[g] = ("sc", se)
                b0, b1 = pr_bank[pr]
                w.release_bank(b0, *pt_prod[g])
                w.release_bank(b1, *pt_prod[g])

            def emit_av(g):
                pr = g % 3
                slots = pt_slots[pr]
                sem_p, cnt_p = pt_prod[g]
                w.wait("tensor", sem_p, cnt_p)
                for j in (0, 1):
                    c = 2 * g + j
                    av_g[g] = w.run("tensor", lambda e, c=c, j=j, slots=slots, hd=hd, ov=ov, nck=nck: e.matmul(
                        ov[0:65, :], vtens[:, c, 65 * hd: 65 * hd + 65],
                        slots[:, j, :],
                        start=(c == 0), stop=(c == nck - 1)), "tc")
                if not _NOFILL:
                    b0 = pr_bank[pr][0]
                    w.run("tensor", lambda e, b0=b0: e.matmul(
                        ps[b0][:, 0:256], ones_all[0:1, 0:128], ktens[0:1, 0, 0:256],
                        start=True, stop=True), None)

            for g in range(ngrp):
                emit_st(g)
                emit_exp(g)
                if g >= 2:
                    emit_av(g - 2)
            emit_av(ngrp - 2)
            emit_av(ngrp - 1)

            w.wait("vector", "tc", av_g[ngrp - 1])
            w.wait("vector", "tc", prev_lbc_mm)
            w.wait("vector", "sc", w.cnt["sc"])

            def _vrec(e, ov=ov):
                with nc.allow_low_precision(reason="softmax 1/l as bf16 matmul operand"):
                    return e.reciprocal(out=praw, in_=ov[64:65, :])
            vrec = w.run("vector", _vrec, "vc")
            last_mask_vc = vrec
            w.wait("tensor", "vc", vrec)
            prev_lbc_mm = w.run("tensor", lambda e, ov=ov: e.matmul(
                ov[64:128, :], ones_all[0:1, 0:64], praw, start=True, stop=True,
                tile_position=(0, 64)), "tc")
            w.wait("scalar", "tc", prev_lbc_mm)
            slbc = w.run("scalar", lambda e, ov=ov: e.copy(out=lbc_sb[:], in_=ov[64:128, :]), "sc")
            w.wait("vector", "sc", slbc)
            va = w.run("vector", lambda e, ov=ov, p=p, oc=oc: e.tensor_mul(
                out=qa[64 * p:64 * (p + 1), oc, :], in0=ov[0:64, :], in1=lbc_sb[:]), "vc")
            w.release_bank(ov_bid, "vc", va)
        mask_use_vc = last_mask_vc
        tile_av_end = w.cnt["tc"]

        a_done_vc = w.cnt["vc"]
        for t in range(4):
            src = x_own[ts_(i * 4 + t, 128), :].rearrange("(a p) h -> p a h", p=128)
            w.wait("sync", "vc", gate["xo_vc"])
            w.wait("sync", "st", gate["xo_st"])
            ldx = w.run("sync", lambda e, src=src: e.dma_start(out=xo[:], in_=src), "ld", 16)
            for oh in range(2):
                bid = 4 + oh
                bank = ps[bid]
                w.use_bank(bid)
                if t == 0 and oh == 0:
                    w.wait("tensor", "vc", a_done_vc)
                for oc in range(8):
                    w.run("tensor", lambda e, oc=oc, t=t, oh=oh, bank=bank: e.matmul(
                        bank, qa[:, oc, ts_(t, 128)], woT[:, oc, ts_(oh, 512)],
                        start=(oc == 0), stop=False), None)
                mmo = w.run("tensor", lambda e, oh=oh, bank=bank: e.matmul(
                    bank, ones_all[0:1, 0:128], bo_row[0:1, ts_(oh, 512)],
                    start=False, stop=True), "tc")
                w.wait("vector", "tc", mmo)
                w.wait("vector", "ld", ldx)
                xslice = xo[:, 0, ts_(oh, 512)]
                vx1 = w.run("vector", lambda e, bank=bank, xslice=xslice: e.tensor_add(
                    out=xslice, in0=bank, in1=xslice), "vc")
                w.release_bank(bid, "vc", vx1)
                w.wait("sync", "vc", vx1)
                row0 = i * QTW + t * 128
                dst = x1_dram[row0: row0 + 128, ts_(oh, 512)]
                w.run("sync", lambda e, dst=dst, xslice=xslice: e.dma_start(
                    out=dst, in_=xslice), "st", 16)
                if debug:
                    ddst = dbg["x1"][row0: row0 + 128, ts_(oh, 512)]
                    w.run("sync", lambda e, ddst=ddst, xslice=xslice: e.dma_start(
                        out=ddst, in_=xslice), "st", 16)
            gate["xo_vc"] = w.cnt["vc"]
            gate["xo_st"] = w.cnt["st"]
    x1_all_st = w.cnt["st"]

    if "c" not in phases:
        if debug:
            w.wait("sync", "st", w.cnt["st"])
            w.wait("gpsimd", "dst", w.cnt["dst"])
        return _replay(nc, w)
    stackB.close()
    stackAB.close()
    stackC = ExitStack()
    enC = stackC.enter_context
    accT = enC(nc.sbuf_tensor("accT", [128, 2, 8, D], F32))
    xh2fT = enC(nc.sbuf_tensor("xh2fT", [128, 2, 8, 1024], BF16))
    act = enC(nc.sbuf_tensor("act", [128, 16, 1024], BF16))
    wgr = enC(nc.sbuf_tensor("wgr", [128, 2, 8, 512], BF16))
    wur = enC(nc.sbuf_tensor("wur", [128, 2, 8, 512], BF16))
    wdr = enC(nc.sbuf_tensor("wdr", [128, 16, D], BF16))
    gsil = enC(nc.sbuf_tensor("gsil", [128, 2, 512], BF16))
    affT = enC(nc.sbuf_tensor("affT", [128, 2, 8, 8], F32))
    rwB = enC(nc.sbuf_tensor("rwB", [128, 64], BF16))
    rw_stage = enC(nc.sbuf_tensor("rw_stage", [64, 128], F32))
    rscr = enC(nc.sbuf_tensor("rscr", [128, 4, 8], F32))

    w.wait("gpsimd", "st", x1_all_st)

    w.wait("sync", "st", x1_all_st)
    rwsrc = rw_in.rearrange("e (a j) -> (e a) j", j=128)
    rwld = w.run("sync", lambda e: e.dma_start(out=rw_stage[:], in_=rwsrc), "ld", 16)
    w.use_bank(6)
    w.wait("tensor", "ld", rwld)
    mmrw = w.run("tensor", lambda e: e.transpose(ps[6][:, 0:64], rw_stage[:], id128f[0:64, 0:64]), "tc")
    w.wait("scalar", "tc", mmrw)
    w.run("scalar", lambda e: e.copy(out=rwB[:], in_=ps[6][:, 0:64]), None)
    rwl = w.run("scalar", lambda e: e.drain(), "tr")
    w.release_bank(6, "tr", rwl)

    gu_mm_done = {}
    dn_mm_done = {}

    def gu_load(half, e_, ic):
        slot = ic % 2
        prev = gu_mm_done.get((half, e_, ic - 2)) or gu_mm_done.get((half, e_ - 1, ic + 2)) \
            or gu_mm_done.get((half - 1, NE - 1, ic + 2))
        if prev:
            w.wait("gpsimd", "tc", prev)
        for t_hbm, ring in ((wg_in, wgr), (wu_in, wur)):
            src = t_hbm[e_, :, ts_(ic, 512)].rearrange("(hc p) i -> p hc i", p=128)
            w.run("gpsimd", lambda e, src=src, ring=ring, slot=slot: e.dma_start(
                out=ring[:, slot, :, :], in_=src), "wd", 16)
        return w.cnt["wd"]

    def wd_load(half, e_):
        prev = dn_mm_done.get((half, e_ - 1)) or dn_mm_done.get((half - 1, e_ + NE - 1))
        if prev:
            w.wait("gpsimd", "tc", prev)
        src = wd_in[e_].rearrange("(ic p) h -> p ic h", p=128)
        w.run("gpsimd", lambda e, src=src: e.dma_start(out=wdr[:], in_=src), "wd", 16)
        return w.cnt["wd"]

    out_st = []
    gsil_gate = {}

    def c_prologue(half):
        acc_h = accT[:, half]
        xh2f_h = xh2fT[:, half]
        aff_h = affT[:, half]
        w.wait("sync", "st", x1_all_st)
        asrc = x1_dram[ts_(half, 1024), :].rearrange("(t p) h -> p t h", p=128)
        lda = w.run("sync", lambda e, asrc=asrc, acc_h=acc_h: e.dma_start(out=acc_h, in_=asrc), "ld", 16)
        w.wait("vector", "ld", lda)
        aff_vc = {}
        for t in range(8):
            w.wait("vector", "tr", gate["xh_tr"][0])
            w.wait("vector", "tr2", gate["xh_tr"][1])
            ln_done = emit_ln(acc_h, xh[:], 1, src_off=t, dst_off=0)
            w.wait("tensor", "vc", ln_done)
            gate["xh_tr"] = emit_transposes(xh[:], xh2f_h, 1, dst_toff=t * 128)
            w.use_bank(4 + t % 2)
            rb = ps[4 + t % 2]
            w.wait("tensor", "tr", gate["xh_tr"][0])
            w.wait("tensor", "tr2", gate["xh_tr"][1])
            if t == 0 and half == 0:
                w.wait("tensor", "tr", rwl)
            for hc in range(8):
                mmr = w.run("tensor", lambda e, hc=hc, t=t, rb=rb, xh2f_h=xh2f_h: e.matmul(
                    rb[:, 0:8], xh2f_h[:, hc, ts_(t, 128)], rwB[:, hc: hc + 57: 8],
                    start=(hc == 0), stop=(hc == 7)), "tc" if hc == 7 else None)
            w.wait("vector", "tc", mmr)
            w.run("vector", lambda e, rb=rb: e.reduce_max(out=rscr[:, 0, 0:1], in_=rb[:, 0:8],
                                                          axis=AX.X, negate=True), "vc")
            vnm = w.cnt["vc"]
            w.wait("scalar", "vc", vnm)
            sex = w.run("scalar", lambda e, rb=rb: e.activation(out=rscr[:, 1, :], in_=rb[:, 0:8],
                                                                func=AF.Exp, bias=rscr[:, 0, 0:1],
                                                                scale=1.0), "sc")
            w.wait("vector", "sc", sex)
            w.run("vector", lambda e: e.reduce_sum(out=rscr[:, 0, 1:2], in_=rscr[:, 1, :], axis=AX.X), "vc")
            w.run("vector", lambda e: e.drain(), None)
            w.run("vector", lambda e: e.reciprocal(out=rscr[:, 0, 2:3], in_=rscr[:, 0, 1:2]), "vc")
            w.run("vector", lambda e: e.drain(), None)
            w.run("vector", lambda e: e.tensor_scalar(out=rscr[:, 1, :], in0=rscr[:, 1, :],
                                                      scalar1=rscr[:, 0, 2:3], scalar2=None,
                                                      op0=ALU.mult), "vc")
            w.run("vector", lambda e: e.drain(), None)
            w.run("vector", lambda e: e.reduce_max(out=rscr[:, 0, 3:4], in_=rscr[:, 1, :], axis=AX.X), "vc")
            w.run("vector", lambda e: e.drain(), None)
            w.run("vector", lambda e: e.tensor_scalar(out=rscr[:, 2, :], in0=rscr[:, 1, :],
                                                      scalar1=rscr[:, 0, 3:4], scalar2=None,
                                                      op0=ALU.is_ge), "vc")
            w.run("vector", lambda e: e.drain(), None)
            w.run("vector", lambda e: e.scalar_tensor_tensor(out=rscr[:, 2, :], in0=rscr[:, 2, :],
                                                             scalar=-2.0, in1=rscr[:, 1, :],
                                                             op0=ALU.mult, op1=ALU.add), "vc")
            w.run("vector", lambda e: e.drain(), None)
            w.run("vector", lambda e: e.reduce_max(out=rscr[:, 0, 4:5], in_=rscr[:, 2, :], axis=AX.X), "vc")
            w.run("vector", lambda e: e.drain(), None)
            w.run("vector", lambda e: e.tensor_scalar(out=rscr[:, 2, :], in0=rscr[:, 1, :],
                                                      scalar1=rscr[:, 0, 4:5], scalar2=None,
                                                      op0=ALU.is_ge), "vc")
            w.run("vector", lambda e: e.tensor_add(out=rscr[:, 0, 5:6], in0=rscr[:, 0, 3:4],
                                                   in1=rscr[:, 0, 4:5]), "vc")
            w.run("vector", lambda e: e.drain(), None)
            w.run("vector", lambda e: e.tensor_mul(out=rscr[:, 3, :], in0=rscr[:, 1, :],
                                                   in1=rscr[:, 2, :]), "vc")
            w.run("vector", lambda e: e.reciprocal(out=rscr[:, 0, 6:7], in_=rscr[:, 0, 5:6]), "vc")
            w.run("vector", lambda e: e.drain(), None)
            aff_vc[t] = w.run("vector", lambda e, t=t, aff_h=aff_h: e.tensor_scalar(
                out=aff_h[:, t, :], in0=rscr[:, 3, :], scalar1=rscr[:, 0, 6:7],
                scalar2=None, op0=ALU.mult), "vc")
            w.release_bank(4 + t % 2, "vc", aff_vc[t])
        if debug:
            w.wait("gpsimd", "vc", aff_vc[7])
            w.run("gpsimd", lambda e, half=half, aff_h=aff_h: e.dma_start(
                out=dbg["aff"][half], in_=aff_h), "dst", 16)
        return acc_h, xh2f_h, aff_h, lda

    def c_experts(half, acc_h, xh2f_h, aff_h, lda):
        gu_lds = {(half, 0, 0): gu_load(half, 0, 0), (half, 0, 1): gu_load(half, 0, 1)}
        wd_lds = {(half, 0): wd_load(half, 0)}
        for e_ in range(c_nexp):
            for ic in range(4):
                ld_cnt = gu_lds.pop((half, e_, ic), None)
                if ld_cnt is None:
                    ld_cnt = gu_load(half, e_, ic)
                w.wait("tensor", "wd", ld_cnt)
                slot = ic % 2
                for isub in range(4):
                    for tt in range(2):
                        j = (isub * 2 + tt) % 2
                        gb, ub = ps[0 + j], ps[2 + j]
                        w.use_bank(0 + j)
                        for hc in range(8):
                            mmg = w.run("tensor", lambda e, hc=hc, isub=isub, tt=tt, slot=slot, gb=gb,
                                        xh2f_h=xh2f_h: e.matmul(
                                gb, wgr[:, slot, hc, ts_(isub, 128)], xh2f_h[:, hc, ts_(tt, 512)],
                                start=(hc == 0), stop=(hc == 7)), "tc" if hc == 7 else None)
                        w.use_bank(2 + j)
                        for hc in range(8):
                            mmu = w.run("tensor", lambda e, hc=hc, isub=isub, tt=tt, slot=slot, ub=ub,
                                        xh2f_h=xh2f_h: e.matmul(
                                ub, wur[:, slot, hc, ts_(isub, 128)], xh2f_h[:, hc, ts_(tt, 512)],
                                start=(hc == 0), stop=(hc == 7)), "tc" if hc == 7 else None)
                        w.wait("scalar", "tc", mmg)
                        w.wait("scalar", "vc", gsil_gate.get(j, 0))
                        sg = w.run("scalar", lambda e, gb=gb, j=j: e.activation(
                            out=gsil[:, j, :], in_=gb, func=AF.Silu, bias=0.0, scale=1.0), "sc")
                        w.release_bank(0 + j, "sc", sg)
                        w.wait("vector", "tc", mmu)
                        w.wait("vector", "sc", sg)
                        vact = w.run("vector", lambda e, ub=ub, j=j, ic=ic, isub=isub, tt=tt: e.tensor_mul(
                            out=act[:, ic * 4 + isub, ts_(tt, 512)], in0=ub, in1=gsil[:, j, :]), "vc")
                        w.release_bank(2 + j, "vc", vact)
                        gsil_gate[j] = vact
                gu_mm_done[(half, e_, ic)] = w.cnt["tc"]
                nxt = (e_, ic + 2) if ic + 2 < 4 else (e_ + 1, ic - 2)
                if nxt[0] < NE and (half,) + nxt not in gu_lds:
                    gu_lds[(half,) + nxt] = gu_load(half, *nxt)
            act_vc = w.cnt["vc"]
            w.wait("tensor", "wd", wd_lds[(half, e_)])
            w.wait("tensor", "vc", act_vc)
            for t in range(8):
                for oh in range(2):
                    bid = 4 + (t * 2 + oh) % 4
                    bank = ps[bid]
                    w.use_bank(bid)
                    for icc in range(16):
                        mmd = w.run("tensor", lambda e, icc=icc, t=t, oh=oh, bank=bank: e.matmul(
                            bank, act[:, icc, ts_(t, 128)], wdr[:, icc, ts_(oh, 512)],
                            start=(icc == 0), stop=(icc == 15)), "tc" if icc == 15 else None)
                    w.wait("vector", "tc", mmd)
                    if e_ == 0:
                        w.wait("vector", "ld", lda)
                    vd = w.run("vector", lambda e, bank=bank, t=t, oh=oh, e_=e_, acc_h=acc_h,
                               aff_h=aff_h: e.scalar_tensor_tensor(
                        out=acc_h[:, t, ts_(oh, 512)], in0=bank, scalar=aff_h[:, t, e_: e_ + 1],
                        in1=acc_h[:, t, ts_(oh, 512)], op0=ALU.mult, op1=ALU.add), "vc")
                    w.release_bank(bid, "vc", vd)
            dn_mm_done[(half, e_)] = w.cnt["tc"]
            if e_ + 1 < NE and (half, e_ + 1) not in wd_lds:
                wd_lds[(half, e_ + 1)] = wd_load(half, e_ + 1)
        acc_free_vc = w.cnt["vc"]
        w.wait("sync", "vc", acc_free_vc)
        dsto = out[ts_(half, 1024), :].rearrange("(t p) h -> p t h", p=128)
        out_st.append(w.run("sync", lambda e, dsto=dsto, acc_h=acc_h: e.dma_start(
            out=dsto, in_=acc_h), "st", 16))

    p0 = c_prologue(0)
    p1 = c_prologue(1)
    c_experts(0, *p0)
    c_experts(1, *p1)

    if debug:
        w.wait("gpsimd", "vc", w.cnt["vc"])
        w.wait("gpsimd", "tr", w.cnt["tr"])
        for icc in range(16):
            w.run("gpsimd", lambda e, icc=icc: e.dma_start(out=dbg["act"][icc], in_=act[:, icc, :]), "dst", 16)
        for hc in range(8):
            w.run("gpsimd", lambda e, hc=hc: e.dma_start(out=dbg["xh2f"][hc], in_=xh2fT[:, 0, hc, :]), "dst", 16)

    if enddbg:
        w.wait("gpsimd", "st", w.cnt["st"])
        for hc in range(8):
            w.run("gpsimd", lambda e, hc=hc: e.dma_start(out=dbg["xh2f0"][hc], in_=xh2fT[:, 0, hc, :]), "dst", 16)
        w.run("gpsimd", lambda e: e.dma_start(out=dbg["aff0"], in_=affT[:, 0]), "dst", 16)
        w.wait("gpsimd", "dst", w.cnt["dst"])

    w.wait("sync", "st", w.cnt["st"])
    if debug:
        w.wait("gpsimd", "dst", w.cnt["dst"])
    return _replay(nc, w)


def _replay(nc, w):
    with nc.Block() as block:
        @block.sync
        def _(eng):
            for f in w.thunks["sync"]:
                f(eng)

        @block.gpsimd
        def _(eng):
            for f in w.thunks["gpsimd"]:
                f(eng)

        @block.vector
        def _(eng):
            for f in w.thunks["vector"]:
                f(eng)

        @block.scalar
        def _(eng):
            for f in w.thunks["scalar"]:
                f(eng)

        @block.tensor
        def _(eng):
            for f in w.thunks["tensor"]:
                f(eng)

    return nc


def make_masks(tiles):
    m = np.zeros((NQT, 8, 128, QTW), np.float32)
    for i, g in enumerate(tiles):
        nck = NCK[i]
        q = g * QTW + np.arange(QTW)[None, :]
        for j in range(8):
            kchunk = nck - 8 + j
            k = kchunk * 128 + np.arange(128)[:, None]
            m[i, j] = (k <= q).astype(np.float32)
    return m.astype(ml_dtypes.bfloat16)


def shard_inputs(inputs):
    x = np.ascontiguousarray(inputs["L0_attn_x"])
    maps = []
    for c in range(8):
        s, h = c // 2, c % 2
        tiles = TILESETS[h]
        x_seq = np.ascontiguousarray(np.asarray(x[s], np.float32))
        x_own = np.ascontiguousarray(
            np.concatenate([x_seq[ts_(g, QTW)] for g in tiles], axis=0))
        maps.append({
            "x_seq": x_seq, "x_own": x_own, "m_in": make_masks(tiles),
            "wq_in": inputs["L0_attn_Wq"], "wk_in": inputs["L0_attn_Wk"],
            "wv_in": inputs["L0_attn_Wv"], "wo_in": inputs["L0_attn_Wo"],
            "bq_in": inputs["L0_attn_bq"], "bk_in": inputs["L0_attn_bk"],
            "bv_in": inputs["L0_attn_bv"], "bo_in": inputs["L0_attn_bo"],
            "rw_in": inputs["L0_ffn_router_w"], "wg_in": inputs["L0_ffn_w_gate"],
            "wu_in": inputs["L0_ffn_w_up"], "wd_in": inputs["L0_ffn_w_down"],
        })
    return maps


def kernel(**inputs):
    inputs = {k: np.asarray(v) for k, v in inputs.items()}
    maps = shard_inputs(inputs)
    nc = build_nc(debug=False)
    res = run_bass_kernel_spmd(nc, maps, core_ids=list(range(8)))
    outf = np.zeros((4, SEQ, D), np.float32)
    for c in range(8):
        s, h = c // 2, c % 2
        o = res.results[c]["out"]
        for i, g in enumerate(TILESETS[h]):
            outf[s, ts_(g, QTW)] = o[ts_(i, QTW)]
    return outf


# version 17
# speedup vs baseline: 1.1400x; 1.1400x over previous
import os
import numpy as np
import ml_dtypes
from contextlib import ExitStack

import concourse.bass as bass
import concourse.mybir as mybir
from concourse.bass_utils import run_bass_kernel_spmd

F32 = mybir.dt.float32
BF16 = mybir.dt.bfloat16
AF = mybir.ActivationFunctionType
ALU = mybir.AluOpType
AX = mybir.AxisListType

NH, DKH, D, SEQ = 16, 64, 1024, 4096
NQT, QTW = 4, 512
OWN = NQT * QTW
NCK = [8, 16, 24, 32]
TILESETS = [[0, 3, 4, 7], [1, 2, 5, 6]]
ISQ = float(8.0 ** -0.5)
NE, DI = 8, 2048
EPS = 1e-5
_NOFILL = os.environ.get("K_NOFILL", "0") == "1"
_NOTR2 = True


def ts_(i, w):
    return slice(i * w, (i + 1) * w)


class Walker:

    def __init__(self, nc, stack):
        self.nc = nc
        self.stack = stack
        self.thunks = {"sync": [], "gpsimd": [], "vector": [], "scalar": [], "tensor": []}
        self.sems = {}
        self.cnt = {}
        self.bank_gate = {}

    def sem(self, name):
        self.sems[name] = self.stack.enter_context(self.nc.semaphore(name))
        self.cnt[name] = 0

    def wait(self, eng, sem, val):
        if val is None or val <= 0:
            return
        h = self.sems[sem]
        self.thunks[eng].append(lambda e, h=h, v=val: e.wait_ge(h, v))

    def run(self, eng, f, sem=None, k=1):
        h = self.sems[sem] if sem else None

        def thunk(e, f=f, h=h, k=k):
            inst = f(e)
            if h is not None:
                inst.then_inc(h, k)

        self.thunks[eng].append(thunk)
        if sem is not None:
            self.cnt[sem] += k
            return self.cnt[sem]
        return None

    def use_bank(self, bid, eng="tensor"):
        g = self.bank_gate.get(bid)
        if g:
            self.wait(eng, g[0], g[1])

    def release_bank(self, bid, sem, cnt):
        self.bank_gate[bid] = (sem, cnt)


def build_nc(debug=False, enddbg=False, phases="abc", c_halves=2, c_nexp=NE, n_tc=8, wprep=True, aparts="lntkvm"):
    nc = bass.Bass("TRN2", target_bir_lowering=False, debug=False,
                   detect_race_conditions=False)

    x_seq = nc.dram_tensor("x_seq", [SEQ, D], F32, kind="ExternalInput").ap()
    x_own = nc.dram_tensor("x_own", [OWN, D], F32, kind="ExternalInput").ap()
    m_in = nc.dram_tensor("m_in", [NQT, 8, 128, QTW], BF16, kind="ExternalInput").ap()
    wq_in = nc.dram_tensor("wq_in", [D, D], F32, kind="ExternalInput").ap()
    wk_in = nc.dram_tensor("wk_in", [D, D], F32, kind="ExternalInput").ap()
    wv_in = nc.dram_tensor("wv_in", [D, D], F32, kind="ExternalInput").ap()
    wo_in = nc.dram_tensor("wo_in", [D, D], F32, kind="ExternalInput").ap()
    bq_in = nc.dram_tensor("bq_in", [D], F32, kind="ExternalInput").ap()
    bk_in = nc.dram_tensor("bk_in", [D], F32, kind="ExternalInput").ap()
    bv_in = nc.dram_tensor("bv_in", [D], F32, kind="ExternalInput").ap()
    bo_in = nc.dram_tensor("bo_in", [D], F32, kind="ExternalInput").ap()
    rw_in = nc.dram_tensor("rw_in", [NE, D], F32, kind="ExternalInput").ap()
    wg_in = nc.dram_tensor("wg_in", [NE, D, DI], F32, kind="ExternalInput").ap()
    wu_in = nc.dram_tensor("wu_in", [NE, D, DI], F32, kind="ExternalInput").ap()
    wd_in = nc.dram_tensor("wd_in", [NE, DI, D], F32, kind="ExternalInput").ap()

    out = nc.dram_tensor("out", [OWN, D], F32, kind="ExternalOutput").ap()
    x1_dram = nc.dram_tensor("x1_scr", [OWN, D], F32, kind="Internal").ap()

    dbg = {}
    if enddbg:
        dbg["xh2f0"] = nc.dram_tensor("dbg_xh2f0", [8, 128, 1024], F32, kind="ExternalOutput").ap()
        dbg["aff0"] = nc.dram_tensor("dbg_aff0", [128, 8, 8], F32, kind="ExternalOutput").ap()
    if debug:
        dbg["k"] = nc.dram_tensor("dbg_k", [8, 128, SEQ], F32, kind="ExternalOutput").ap()
        dbg["v"] = nc.dram_tensor("dbg_v", [32, 128, NH * 65], F32, kind="ExternalOutput").ap()
        dbg["km2"] = nc.dram_tensor("dbg_km2", [16, 1], F32, kind="ExternalOutput").ap()
        dbg["x1"] = nc.dram_tensor("dbg_x1", [OWN, D], F32, kind="ExternalOutput").ap()
        dbg["aff"] = nc.dram_tensor("dbg_aff", [2, 128, 8, 8], F32, kind="ExternalOutput").ap()
        dbg["act"] = nc.dram_tensor("dbg_act", [16, 128, 1024], F32, kind="ExternalOutput").ap()
        dbg["xhf"] = nc.dram_tensor("dbg_xhf", [8, 128, 512], F32, kind="ExternalOutput").ap()
        dbg["xh"] = nc.dram_tensor("dbg_xh", [128, 2, 1024], F32, kind="ExternalOutput").ap()
        dbg["xh2f"] = nc.dram_tensor("dbg_xh2f", [8, 128, 1024], F32, kind="ExternalOutput").ap()

    stack = ExitStack()
    nc._walker_stack = stack

    w = Walker(nc, stack)
    for s in ("ld", "tr", "tr2", "wd", "st", "dst", "tc", "vc", "sc", "gc"):
        w.sem(s)

    en = stack.enter_context

    xh = en(nc.sbuf_tensor("xh", [128, 2, D], F32))
    ones_all = en(nc.sbuf_tensor("ones_all", [16, 128], BF16))
    id16 = en(nc.sbuf_tensor("id16", [16, 16], BF16))
    id128f = en(nc.sbuf_tensor("id128f", [128, 128], F32))
    bqk_c = en(nc.sbuf_tensor("bqk_c", [128, 2, 8], F32))
    bn6 = en(nc.sbuf_tensor("bn6", [128, 12], F32))
    st4 = en(nc.sbuf_tensor("st4", [128, 4], F32))
    lnt = en(nc.sbuf_tensor("lnt", [128, 8], F32))
    nrmL = en(nc.sbuf_tensor("nrmL", [128, 8, 16], BF16))
    psA = en(nc.psum_tensor("psA", [128, 2, 512], F32))
    psB = en(nc.psum_tensor("psB", [128, 2, 512], F32))
    psD = en(nc.psum_tensor("psD", [128, 2, 512], F32))
    psC = en(nc.psum_tensor("psC", [128, 2, 512], F32))
    pairs3 = [psA[:], psB[:], psC[:]]
    ps = [psA[:, 0, :], psA[:, 1, :], psB[:, 0, :], psB[:, 1, :],
          psD[:, 0, :], psD[:, 1, :], psC[:, 0, :], psC[:, 1, :]]

    stackAB = ExitStack()
    enAB = stackAB.enter_context
    xo = enAB(nc.sbuf_tensor("xo", [128, 1, D], F32))
    idt128 = xo[:, 0, 0:128]
    xhf = enAB(nc.sbuf_tensor("xhf", [128, 8, QTW], BF16))
    ktens = enAB(nc.sbuf_tensor("ktens", [128, 8, SEQ], BF16))
    vtens = enAB(nc.sbuf_tensor("vtens", [128, 32, NH * 65], BF16))
    km2 = enAB(nc.sbuf_tensor("km2", [16, 1], F32))
    cm2 = enAB(nc.sbuf_tensor("cm2", [16, 1], F32))
    ksq = enAB(nc.sbuf_tensor("ksq", [128, 2, 512], BF16))
    idtmp = ksq[0:16, 1, 0:16]
    cmdiag = ksq[0:16, 0, 0:16]
    vpad = vtens[:].rearrange("p c (h e) -> p c h e", e=65)

    for j, b in enumerate((bq_in, bk_in)):
        src = b.rearrange("(oc p) -> p oc", p=128)
        w.run("sync", lambda e, j=j, src=src: e.dma_start(out=bqk_c[:, j, :], in_=src,
                                                          allow_slow_non_contiguous=True), "ld", 16)
    bld = w.cnt["ld"]
    w.run("gpsimd", lambda e: e.iota(idtmp, pattern=[[1, 16]], base=0, channel_multiplier=-1,
                                     allow_small_or_imprecise_dtypes=True), "gc")
    w.run("gpsimd", lambda e: e.iota(idt128, pattern=[[1, 128]], base=0, channel_multiplier=-1,
                                     allow_small_or_imprecise_dtypes=True), "gc")
    iota_done = w.cnt["gc"]

    w.run("vector", lambda e: e.memset(ones_all[:], 1.0), "vc")
    w.run("vector", lambda e: e.memset(km2[:], 0.0), "vc")
    w.wait("vector", "ld", bld)
    for j in range(2):
        w.run("vector", lambda e, j=j: e.tensor_scalar(out=bqk_c[:, j, :], in0=bqk_c[:, j, :],
                                                       scalar1=ISQ, scalar2=None, op0=ALU.mult), "vc")
    w.wait("vector", "gc", iota_done)
    w.run("vector", lambda e: e.tensor_single_scalar(out=id16[:], in_=idtmp, scalar=0.0,
                                                     op=ALU.is_equal), "vc")
    w.run("vector", lambda e: e.tensor_single_scalar(out=id128f[:], in_=idt128, scalar=0.0,
                                                     op=ALU.is_equal), "vc")
    w.run("vector", lambda e: e.memset(nrmL[:], 0.0), "vc")
    for oc in range(8):
        w.run("vector", lambda e, oc=oc: e.memset(nrmL[0:64, oc, 2 * oc: 2 * oc + 1], 1.0), "vc")
        w.run("vector", lambda e, oc=oc: e.memset(nrmL[64:128, oc, 2 * oc + 1: 2 * oc + 2], 1.0), "vc")
    setup_v = w.cnt["vc"]


    def emit_ln(src_ap, dst_ap, n_ts, src_off=0, dst_off=0):
        for t in range(n_ts):
            si, di = src_off + t, dst_off + t
            w.run("vector", lambda e, si=si: e.bn_stats(out=bn6[:, 0:6], in_=src_ap[:, si, 0:512]), "vc")
            w.run("vector", lambda e, si=si: e.bn_stats(out=bn6[:, 6:12], in_=src_ap[:, si, 512:1024]), "vc")
            w.run("vector", lambda e: e.bn_aggr(out=st4[:, 0:2], in_=bn6[:, 0:6]), "vc")
            w.run("vector", lambda e: e.bn_aggr(out=st4[:, 2:4], in_=bn6[:, 6:12]), "vc")
            w.run("vector", lambda e: e.drain(), None)
            w.run("vector", lambda e: e.tensor_scalar(out=lnt[:, 1:2], in0=st4[:, 0:1],
                                                      scalar1=st4[:, 2:3], scalar2=0.5,
                                                      op0=ALU.add, op1=ALU.mult), "vc")
            w.run("vector", lambda e: e.tensor_sub(out=lnt[:, 2:3], in0=st4[:, 0:1], in1=st4[:, 2:3]), "vc")
            w.run("vector", lambda e: e.tensor_scalar(out=lnt[:, 5:6], in0=st4[:, 1:2],
                                                      scalar1=st4[:, 3:4], scalar2=0.5,
                                                      op0=ALU.add, op1=ALU.mult), "vc")
            w.run("vector", lambda e: e.drain(), None)
            w.run("vector", lambda e: e.tensor_scalar(out=lnt[:, 3:4], in0=lnt[:, 2:3],
                                                      scalar1=lnt[:, 2:3], scalar2=4.0 * EPS,
                                                      op0=ALU.mult, op1=ALU.add), "vc")
            w.run("vector", lambda e: e.drain(), None)
            vvar = w.run("vector", lambda e: e.tensor_scalar(out=lnt[:, 6:7], in0=lnt[:, 3:4], scalar1=0.25,
                                                             scalar2=lnt[:, 5:6], op0=ALU.mult, op1=ALU.add), "vc")
            w.wait("scalar", "vc", vvar)
            w.run("scalar", lambda e: e.activation(out=lnt[:, 7:8], in_=lnt[:, 6:7], func=AF.Ln,
                                                   bias=0.0, scale=1.0), "sc")
            w.run("scalar", lambda e: e.drain(), None)
            sr = w.run("scalar", lambda e: e.activation(out=lnt[:, 6:7], in_=lnt[:, 7:8], func=AF.Exp,
                                                        bias=0.0, scale=-0.5), "sc")
            w.wait("vector", "sc", sr)
            w.run("vector", lambda e, si=si, di=di: e.tensor_scalar(
                out=dst_ap[:, di, :], in0=src_ap[:, si, :],
                scalar1=lnt[:, 1:2], scalar2=lnt[:, 6:7],
                op0=ALU.subtract, op1=ALU.mult), "vc")
        return w.cnt["vc"]

    def emit_transposes(src_ap, dst_ap, n_ts, src_off=0, dst_toff=0, banks=(5, 6, 7)):
        for hc in range(8):
            bid = banks[hc % len(banks)]
            bank = ps[bid]
            w.use_bank(bid)
            for t in range(n_ts):
                mmt = w.run("tensor", lambda e, t=t, hc=hc, bank=bank: e.transpose(
                    bank[:, ts_(t, 128)], src_ap[:, t, ts_(hc, 128)], id128f[:]),
                    "tc" if t == n_ts - 1 else None)
            w.wait("scalar", "tc", mmt)
            dst = dst_ap[:, hc, dst_toff + src_off * 128: dst_toff + (src_off + n_ts) * 128]
            w.run("scalar", lambda e, dst=dst, bank=bank, n_ts=n_ts: e.copy(
                out=dst, in_=bank[:, 0: n_ts * 128]), None)
            ev = w.run("scalar", lambda e: e.drain(), "tr")
            w.release_bank(bid, "tr", ev)
        return (w.cnt["tr"], w.cnt["tr2"])

    def emit_weight_prep(w_hbm, wtmp, wT):
        for g in range(4):
            src = w_hbm[ts_(g, 256), :].rearrange("(oc p) h -> p oc h", p=128)
            w.wait("sync", "tr", w.cnt["tr"])
            dcnt = w.run("sync", lambda e, src=src: e.dma_start(out=wtmp[:], in_=src), "ld", 16)
            w.wait("tensor", "ld", dcnt)
            for hc in range(8):
                bid = (6, 7)[hc % 2]
                bank = ps[bid]
                w.use_bank(bid)
                for oc in range(2):
                    mmt = w.run("tensor", lambda e, hc=hc, oc=oc, bank=bank: e.transpose(
                        bank[:, ts_(oc, 128)], wtmp[:, oc, ts_(hc, 128)], id128f[:]),
                        "tc" if oc == 1 else None)
                w.wait("scalar", "tc", mmt)
                dst = wT[:, hc, g * 256: (g + 1) * 256]
                ev = w.run("scalar", lambda e, dst=dst, bank=bank: e.copy(
                    out=dst, in_=bank[:, 0:256]), "tr")
                w.release_bank(bid, "tr", ev)
        return (w.cnt["tr"], w.cnt["tr2"])

    gate = {"xo_vc": setup_v, "xh_tr": (0, 0), "xhf_tc": 0, "xo_st": 0}

    def emit_x_tile(rows_ap, dst=None, xhf_gate_tc=None):
        dst = xhf if dst is None else dst
        gtc = gate["xhf_tc"] if xhf_gate_tc is None else xhf_gate_tc
        for u in range(4):
            src = rows_ap[ts_(u, 128), :].rearrange("(t p) h -> p t h", p=128)
            w.wait("sync", "vc", gate["xo_vc"])
            w.wait("sync", "st", gate["xo_st"])
            ldx = w.run("sync", lambda e, src=src: e.dma_start(out=xo[:], in_=src), "ld", 16)
            w.wait("vector", "ld", ldx)
            if u % 2 == 0:
                w.wait("vector", "tr", gate["xh_tr"][0])
            ln_done = emit_ln(xo[:], xh[:], 1, dst_off=u % 2)
            gate["xo_vc"] = ln_done
            if u == 1:
                w.wait("tensor", "tc", gtc)
            if u % 2 == 1:
                w.wait("tensor", "vc", ln_done)
                gate["xh_tr"] = emit_transposes(xh[:], dst, 2, src_off=0,
                                                dst_toff=(u // 2) * 256)
        return gate["xh_tr"]

    stackA = ExitStack()
    enA = stackA.enter_context
    wkT = enA(nc.sbuf_tensor("wkT", [128, 8, D], BF16))
    wvT = enA(nc.sbuf_tensor("wvT", [128, 8, D], BF16))
    wtmpA = enA(nc.sbuf_tensor("wtmpA", [128, 2, D], F32))
    bv_row = enA(nc.sbuf_tensor("bv_row", [1, D], BF16))
    xhf2 = enA(nc.sbuf_tensor("xhf2", [128, 8, QTW], BF16))

    w.run("gpsimd", lambda e: e.dma_start(out=bv_row[:], in_=bv_in.rearrange("(o h) -> o h", o=1)), "wd", 16)
    if wprep:
        emit_weight_prep(wk_in, wtmpA, wkT)
        trv, trv2 = emit_weight_prep(wv_in, wtmpA, wvT)
    else:
        trv, trv2 = w.cnt["tr"], w.cnt["tr2"]

    vones = w.run("vector", lambda e: e.memset(vpad[:, :, :, 64:65], 1.0), "vc")

    w.wait("tensor", "tr", trv)
    w.wait("tensor", "tr2", trv2)
    w.wait("tensor", "vc", setup_v)
    w.wait("scalar", "vc", setup_v)
    wd_bias = w.cnt["wd"]

    xhf_mm_done = {0: 0, 1: 0}
    for tc in range(n_tc):
        buf = xhf if tc % 2 == 0 else xhf2
        if "l" in aparts:
            emit_x_tile(x_seq[ts_(tc, 512), :], dst=buf, xhf_gate_tc=xhf_mm_done[tc % 2])
        w.wait("tensor", "tr", gate["xh_tr"][0])
        w.wait("tensor", "tr2", gate["xh_tr"][1])

        kev_sc = {}
        if "k" not in aparts:
            continue
        for oc in range(8):
            bid = oc % 2
            bank = ps[bid]
            w.use_bank(bid)
            for hc in range(8):
                mmk = w.run("tensor", lambda e, hc=hc, oc=oc, bank=bank, buf=buf: e.matmul(
                    bank, wkT[:, hc, ts_(oc, 128)], buf[:, hc, :],
                    start=(hc == 0), stop=(hc == 7)), "tc" if hc == 7 else None)
            w.wait("scalar", "tc", mmk)
            kev_sc[oc] = w.run("scalar", lambda e, tc=tc, oc=oc, bank=bank: e.activation(
                out=ktens[:, oc, ts_(tc, 512)], in_=bank, func=AF.Identity,
                bias=bqk_c[:, 1, oc: oc + 1], scale=ISQ), "sc")
            w.release_bank(bid, "sc", kev_sc[oc])

        if "v" not in aparts:
            gate["xhf_tc"] = w.cnt["tc"]
            continue
        w.wait("tensor", "vc", vones)
        for t in range(4):
            for oh in range(2):
                bid = 2 + (t * 2 + oh) % 2
                bank = ps[bid]
                w.use_bank(bid)
                for hc in range(8):
                    w.run("tensor", lambda e, hc=hc, t=t, oh=oh, bank=bank, buf=buf: e.matmul(
                        bank, buf[:, hc, ts_(t, 128)], wvT[:, hc, ts_(oh, 512)],
                        start=(hc == 0), stop=False), None)
                mmv = w.run("tensor", lambda e, oh=oh, bank=bank: e.matmul(
                    bank, ones_all[0:1, 0:128], bv_row[0:1, ts_(oh, 512)],
                    start=False, stop=True), "tc")
                w.wait("vector", "tc", mmv)
                kc = tc * 4 + t
                dst = vpad[:, kc, ts_(oh, 8), 0:64]
                srcp = bank.rearrange("p (h d) -> p h d", d=64)
                vcnt = w.run("vector", lambda e, dst=dst, srcp=srcp: e.tensor_copy(
                    out=dst, in_=srcp), "vc")
                w.release_bank(bid, "vc", vcnt)
        xhf_mm_done[tc % 2] = w.cnt["tc"]
        gate["xhf_tc"] = w.cnt["tc"]

        if "m" not in aparts:
            continue
        w.use_bank(4)
        nrm_mm = {}
        for oc in range(8):
            sq = ksq[:, oc % 2, :]
            w.wait("vector", "sc", kev_sc[oc])
            if oc >= 2:
                w.wait("vector", "tc", nrm_mm[oc - 2])
            vsq = w.run("vector", lambda e, sq=sq, oc=oc, tc=tc: e.tensor_mul(
                out=sq, in0=ktens[:, oc, ts_(tc, 512)], in1=ktens[:, oc, ts_(tc, 512)]), "vc")
            w.wait("tensor", "vc", vsq)
            nrm_mm[oc] = w.run("tensor", lambda e, sq=sq, oc=oc: e.matmul(
                ps[4][0:16, :], nrmL[:, oc, :], sq,
                start=(oc == 0), stop=(oc == 7)), "tc")
        w.wait("vector", "tc", nrm_mm[7])
        w.run("vector", lambda e: e.reduce_max(out=lnt[0:16, 0:1], in_=ps[4][0:16, :], axis=AX.X), "vc")
        w.run("vector", lambda e: e.drain(), None)
        nm = w.run("vector", lambda e: e.tensor_max(out=km2[:], in0=km2[:], in1=lnt[0:16, 0:1]), "vc")
        w.release_bank(4, "vc", nm)

    phaseA_vc, phaseA_sc = w.cnt["vc"], w.cnt["sc"]

    if debug:
        w.wait("gpsimd", "sc", phaseA_sc)
        w.wait("gpsimd", "vc", phaseA_vc)
        w.wait("gpsimd", "tr", w.cnt["tr"])
        for hc in range(8):
            w.run("gpsimd", lambda e, hc=hc: e.dma_start(out=dbg["xhf"][hc], in_=xhf[:, hc, :]), "dst", 16)
        w.run("gpsimd", lambda e: e.dma_start(out=dbg["xh"], in_=xh[:]), "dst", 16)
        for oc in range(8):
            w.run("gpsimd", lambda e, oc=oc: e.dma_start(out=dbg["k"][oc], in_=ktens[:, oc, :]), "dst", 16)
        for kc in range(32):
            w.run("gpsimd", lambda e, kc=kc: e.dma_start(out=dbg["v"][kc], in_=vtens[:, kc, :]), "dst", 16)
        w.run("gpsimd", lambda e: e.dma_start(out=dbg["km2"], in_=km2[:]), "dst", 16)

    if "b" not in phases:
        if debug:
            w.wait("sync", "st", w.cnt["st"])
            w.wait("gpsimd", "dst", w.cnt["dst"])
        return _replay(nc, w)
    stackA.close()
    stackB = ExitStack()
    enB = stackB.enter_context
    stackB2 = ExitStack()
    wqT = enB(nc.sbuf_tensor("wqT", [128, 8, D], BF16))
    woT = enB(nc.sbuf_tensor("woT", [128, 8, D], BF16))
    bo_row = enB(nc.sbuf_tensor("bo_row", [1, D], BF16))
    wtmpB = stackB2.enter_context(nc.sbuf_tensor("wtmpB", [128, 2, D], F32))

    w.wait("gpsimd", "tc", gate["xhf_tc"])
    w.wait("sync", "tc", gate["xhf_tc"])
    w.run("gpsimd", lambda e: e.dma_start(out=bo_row[:], in_=bo_in.rearrange("(o h) -> o h", o=1)), "wd", 16)
    emit_weight_prep(wq_in, wtmpB, wqT)
    trq, trq2 = emit_weight_prep(wo_in, wtmpB, woT)
    stackB2.close()

    qa = enB(nc.sbuf_tensor("qa", [128, 8, QTW], BF16))
    ptring = enB(nc.sbuf_tensor("ptring", [128, 4, QTW], BF16))
    praw = xhf[0:1, 7, :]
    lbc_sb = enB(nc.sbuf_tensor("lbc_sb", [64, QTW], BF16))
    cnegb = enB(nc.sbuf_tensor("cnegb", [128, 16], F32))
    mslot = enB(nc.sbuf_tensor("mslot", [128, 8, QTW], BF16))

    w.wait("tensor", "tr", trq)
    w.wait("tensor", "tr2", trq2)
    mask_use_vc = 0
    tile_av_end = 0
    last_mask_vc = 0
    prev_lbc_mm = 0
    phaseA_end_tc = gate["xhf_tc"]

    for i in range(NQT):
        nck = NCK[i]
        if i == 0:
            w.wait("sync", "tc", phaseA_end_tc)
        w.wait("sync", "vc", mask_use_vc)
        msrc = m_in[i].rearrange("c p j -> p c j")
        mld = w.run("sync", lambda e, msrc=msrc: e.dma_start(out=mslot[:], in_=msrc), "ld", 16)

        emit_x_tile(x_own[ts_(i, QTW), :])
        w.wait("tensor", "tr", gate["xh_tr"][0])
        w.wait("tensor", "tr2", gate["xh_tr"][1])

        qev_sc = {}
        for oc in range(8):
            bid = oc % 2
            bank = ps[bid]
            w.use_bank(bid)
            for hc in range(8):
                mmq = w.run("tensor", lambda e, hc=hc, oc=oc, bank=bank: e.matmul(
                    bank, wqT[:, hc, ts_(oc, 128)], xhf[:, hc, :],
                    start=(hc == 0), stop=(hc == 7)), "tc" if hc == 7 else None)
            w.wait("scalar", "tc", mmq)
            qev_sc[oc] = w.run("scalar", lambda e, oc=oc, bank=bank: e.activation(
                out=qa[:, oc, :], in_=bank, func=AF.Identity,
                bias=bqk_c[:, 0, oc: oc + 1], scale=ISQ), "sc")
            w.release_bank(bid, "sc", qev_sc[oc])
        gate["xhf_tc"] = w.cnt["tc"]

        w.use_bank(4)
        nrm_mm = {}
        for oc in range(8):
            sq = ksq[:, oc % 2, :]
            w.wait("vector", "sc", qev_sc[oc])
            if oc < 2:
                w.wait("vector", "tc", tile_av_end)
            if oc >= 2:
                w.wait("vector", "tc", nrm_mm[oc - 2])
            vsq = w.run("vector", lambda e, sq=sq, oc=oc: e.tensor_mul(
                out=sq, in0=qa[:, oc, :], in1=qa[:, oc, :]), "vc")
            w.wait("tensor", "vc", vsq)
            nrm_mm[oc] = w.run("tensor", lambda e, sq=sq, oc=oc: e.matmul(
                ps[4][0:16, :], nrmL[:, oc, :], sq,
                start=(oc == 0), stop=(oc == 7)), "tc")
        w.wait("vector", "tc", nrm_mm[7])
        w.run("vector", lambda e: e.reduce_max(out=lnt[0:16, 0:1], in_=ps[4][0:16, :], axis=AX.X), "vc")
        w.run("vector", lambda e: e.drain(), None)
        vq = w.run("vector", lambda e: e.tensor_max(out=cm2[:], in0=km2[:], in1=lnt[0:16, 0:1]), "vc")
        w.release_bank(4, "vc", vq)
        w.run("vector", lambda e: e.drain(), None)
        vcd = w.run("vector", lambda e: e.tensor_scalar(out=cmdiag, in0=id16[:],
                                                        scalar1=cm2[0:16, 0:1], scalar2=-1.0,
                                                        op0=ALU.mult, op1=ALU.mult), "vc")
        w.use_bank(5)
        w.wait("tensor", "vc", vcd)
        mmc = w.run("tensor", lambda e: e.matmul(ps[5][:, 0:16], ones_all[0:16, 0:128], cmdiag,
                                                 start=True, stop=True), "tc")
        w.wait("scalar", "tc", mmc)
        scb = w.run("scalar", lambda e: e.copy(out=cnegb[:], in_=ps[5][:, 0:16]), "sc")
        w.release_bank(5, "sc", scb)
        w.run("scalar", lambda e: e.drain(), None)

        w.wait("scalar", "ld", mld)
        w.wait("vector", "ld", mld)
        w.wait("scalar", "sc", scb)
        pair_banks = {0: (0, 1), 1: (2, 3), 2: (6, 7)}
        pt_slots = {0: ptring[:, 0:2, :], 1: ptring[:, 2:4, :], 2: ksq[:, 0:2, :]}
        for hp in range(NH // 2):
            oc = hp
            ngrp = nck // 2
            w.wait("tensor", "sc", qev_sc[oc])
            w.use_bank(4)
            w.use_bank(5)
            st_done = {}
            pt_prod = {}
            pair_last_av = {}

            def st_step(g):
                pa, pb = (2 * g) % 3, (2 * g + 1) % 3
                for pr, j in ((0, 0), (1, 0), (0, 1), (1, 1)):
                    pair = pa if pr == 0 else pb
                    bid = pair_banks[pair][j]
                    c = 2 * g + j
                    w.use_bank(bid)
                    st_done[(g, pr, j)] = w.run(
                        "tensor", lambda e, c=c, oc=oc, pr=pr, bid=bid: e.matmul(
                            ps[bid], ktens[64 * pr:64 * (pr + 1), oc, ts_(c, 128)],
                            qa[64 * pr:64 * (pr + 1), oc, :],
                            start=True, stop=True, tile_position=(64 * pr, 0)), "tc")

            def exp_step(g):
                nonlocal last_mask_vc
                pa, pb = (2 * g) % 3, (2 * g + 1) % 3
                masked = 2 * g >= nck - 8
                for pr, pair in ((0, pa), (1, pb)):
                    hd = 2 * hp + pr
                    slots = pt_slots[pair]
                    w.wait("scalar", "tc", st_done[(g, pr, 1)])
                    w.wait("scalar", "tc", pair_last_av.get(pair, 0))
                    se = w.run("scalar", lambda e, pair=pair, hd=hd, slots=slots: e.activation(
                        out=slots, in_=pairs3[pair], func=AF.Exp,
                        bias=cnegb[:, hd: hd + 1], scale=1.0), "sc")
                    if masked:
                        w.wait("vector", "sc", se)
                        for j in (0, 1):
                            c = 2 * g + j
                            last_mask_vc = w.run(
                                "vector", lambda e, c=c, j=j, slots=slots, nck=nck: e.tensor_mul(
                                    out=slots[:, j, :], in0=slots[:, j, :],
                                    in1=mslot[:, c - (nck - 8), :]), "vc")
                        pt_prod[(g, pr)] = ("vc", last_mask_vc)
                    else:
                        pt_prod[(g, pr)] = ("sc", se)
                    for bid in pair_banks[pair]:
                        w.release_bank(bid, *pt_prod[(g, pr)])

            def av_step(g):
                pa, pb = (2 * g) % 3, (2 * g + 1) % 3
                for pr, pair in ((0, pa), (1, pb)):
                    hd = 2 * hp + pr
                    slots = pt_slots[pair]
                    sem_p, cnt_p = pt_prod[(g, pr)]
                    w.wait("tensor", sem_p, cnt_p)
                    ov = ps[4 + pr]
                    for j in (0, 1):
                        c = 2 * g + j
                        cnt = w.run("tensor", lambda e, c=c, j=j, slots=slots, hd=hd, ov=ov,
                                    nck=nck: e.matmul(
                            ov[0:65, :], vtens[:, c, 65 * hd: 65 * hd + 65],
                            slots[:, j, :],
                            start=(c == 0), stop=(c == nck - 1)), "tc")
                    pair_last_av[pair] = cnt

            for g in range(ngrp):
                st_step(g)
                exp_step(g)
                if g >= 1:
                    av_step(g - 1)
            av_step(ngrp - 1)

            prev_va = None
            for pr in (0, 1):
                hd = 2 * hp + pr
                ov = ps[4 + pr]
                w.wait("vector", "tc", pair_last_av[(2 * (ngrp - 1) + pr) % 3])
                w.wait("vector", "tc", prev_lbc_mm)
                w.wait("vector", "sc", w.cnt["sc"])

                def _vrec(e, ov=ov):
                    with nc.allow_low_precision(reason="softmax 1/l as bf16 matmul operand"):
                        return e.reciprocal(out=praw, in_=ov[64:65, :])
                vrec = w.run("vector", _vrec, "vc")
                last_mask_vc = vrec
                w.wait("tensor", "vc", vrec)
                prev_lbc_mm = w.run("tensor", lambda e, ov=ov: e.matmul(
                    ov[64:128, :], ones_all[0:1, 0:64], praw, start=True, stop=True,
                    tile_position=(0, 64)), "tc")
                w.wait("scalar", "tc", prev_lbc_mm)
                slbc = w.run("scalar", lambda e, ov=ov: e.copy(out=lbc_sb[:], in_=ov[64:128, :]), "sc")
                w.wait("vector", "sc", slbc)
                va = w.run("vector", lambda e, ov=ov, pr=pr, oc=oc: e.tensor_mul(
                    out=qa[64 * pr:64 * (pr + 1), oc, :], in0=ov[0:64, :], in1=lbc_sb[:]), "vc")
                w.release_bank(4 + pr, "vc", va)
        mask_use_vc = last_mask_vc
        tile_av_end = w.cnt["tc"]

        a_done_vc = w.cnt["vc"]
        for t in range(4):
            src = x_own[ts_(i * 4 + t, 128), :].rearrange("(a p) h -> p a h", p=128)
            w.wait("sync", "vc", gate["xo_vc"])
            w.wait("sync", "st", gate["xo_st"])
            ldx = w.run("sync", lambda e, src=src: e.dma_start(out=xo[:], in_=src), "ld", 16)
            for oh in range(2):
                bid = 4 + oh
                bank = ps[bid]
                w.use_bank(bid)
                if t == 0 and oh == 0:
                    w.wait("tensor", "vc", a_done_vc)
                for oc in range(8):
                    w.run("tensor", lambda e, oc=oc, t=t, oh=oh, bank=bank: e.matmul(
                        bank, qa[:, oc, ts_(t, 128)], woT[:, oc, ts_(oh, 512)],
                        start=(oc == 0), stop=False), None)
                mmo = w.run("tensor", lambda e, oh=oh, bank=bank: e.matmul(
                    bank, ones_all[0:1, 0:128], bo_row[0:1, ts_(oh, 512)],
                    start=False, stop=True), "tc")
                w.wait("vector", "tc", mmo)
                w.wait("vector", "ld", ldx)
                xslice = xo[:, 0, ts_(oh, 512)]
                vx1 = w.run("vector", lambda e, bank=bank, xslice=xslice: e.tensor_add(
                    out=xslice, in0=bank, in1=xslice), "vc")
                w.release_bank(bid, "vc", vx1)
                w.wait("sync", "vc", vx1)
                row0 = i * QTW + t * 128
                dst = x1_dram[row0: row0 + 128, ts_(oh, 512)]
                w.run("sync", lambda e, dst=dst, xslice=xslice: e.dma_start(
                    out=dst, in_=xslice), "st", 16)
                if debug:
                    ddst = dbg["x1"][row0: row0 + 128, ts_(oh, 512)]
                    w.run("sync", lambda e, ddst=ddst, xslice=xslice: e.dma_start(
                        out=ddst, in_=xslice), "st", 16)
            gate["xo_vc"] = w.cnt["vc"]
            gate["xo_st"] = w.cnt["st"]
    x1_all_st = w.cnt["st"]

    if "c" not in phases:
        if debug:
            w.wait("sync", "st", w.cnt["st"])
            w.wait("gpsimd", "dst", w.cnt["dst"])
        return _replay(nc, w)
    stackB.close()
    stackAB.close()
    stackC = ExitStack()
    enC = stackC.enter_context
    accT = enC(nc.sbuf_tensor("accT", [128, 2, 8, D], F32))
    xh2fT = enC(nc.sbuf_tensor("xh2fT", [128, 2, 8, 1024], BF16))
    act = enC(nc.sbuf_tensor("act", [128, 16, 1024], BF16))
    wgr = enC(nc.sbuf_tensor("wgr", [128, 2, 8, 512], BF16))
    wur = enC(nc.sbuf_tensor("wur", [128, 2, 8, 512], BF16))
    wdr = enC(nc.sbuf_tensor("wdr", [128, 16, D], BF16))
    gsil = enC(nc.sbuf_tensor("gsil", [128, 2, 512], BF16))
    affT = enC(nc.sbuf_tensor("affT", [128, 2, 8, 8], F32))
    rwB = enC(nc.sbuf_tensor("rwB", [128, 64], BF16))
    rw_stage = enC(nc.sbuf_tensor("rw_stage", [64, 128], F32))
    rscr = enC(nc.sbuf_tensor("rscr", [128, 4, 8], F32))

    w.wait("gpsimd", "st", x1_all_st)

    w.wait("sync", "st", x1_all_st)
    rwsrc = rw_in.rearrange("e (a j) -> (e a) j", j=128)
    rwld = w.run("sync", lambda e: e.dma_start(out=rw_stage[:], in_=rwsrc), "ld", 16)
    w.use_bank(6)
    w.wait("tensor", "ld", rwld)
    mmrw = w.run("tensor", lambda e: e.transpose(ps[6][:, 0:64], rw_stage[:], id128f[0:64, 0:64]), "tc")
    w.wait("scalar", "tc", mmrw)
    w.run("scalar", lambda e: e.copy(out=rwB[:], in_=ps[6][:, 0:64]), None)
    rwl = w.run("scalar", lambda e: e.drain(), "tr")
    w.release_bank(6, "tr", rwl)

    gu_mm_done = {}
    dn_mm_done = {}

    def gu_load(half, e_, ic):
        slot = ic % 2
        prev = gu_mm_done.get((half, e_, ic - 2)) or gu_mm_done.get((half, e_ - 1, ic + 2)) \
            or gu_mm_done.get((half - 1, NE - 1, ic + 2))
        if prev:
            w.wait("gpsimd", "tc", prev)
        for t_hbm, ring in ((wg_in, wgr), (wu_in, wur)):
            src = t_hbm[e_, :, ts_(ic, 512)].rearrange("(hc p) i -> p hc i", p=128)
            w.run("gpsimd", lambda e, src=src, ring=ring, slot=slot: e.dma_start(
                out=ring[:, slot, :, :], in_=src), "wd", 16)
        return w.cnt["wd"]

    def wd_load(half, e_):
        prev = dn_mm_done.get((half, e_ - 1)) or dn_mm_done.get((half - 1, e_ + NE - 1))
        if prev:
            w.wait("gpsimd", "tc", prev)
        src = wd_in[e_].rearrange("(ic p) h -> p ic h", p=128)
        w.run("gpsimd", lambda e, src=src: e.dma_start(out=wdr[:], in_=src), "wd", 16)
        return w.cnt["wd"]

    out_st = []
    gsil_gate = {}

    def c_prologue(half):
        acc_h = accT[:, half]
        xh2f_h = xh2fT[:, half]
        aff_h = affT[:, half]
        w.wait("sync", "st", x1_all_st)
        asrc = x1_dram[ts_(half, 1024), :].rearrange("(t p) h -> p t h", p=128)
        lda = w.run("sync", lambda e, asrc=asrc, acc_h=acc_h: e.dma_start(out=acc_h, in_=asrc), "ld", 16)
        w.wait("vector", "ld", lda)
        aff_vc = {}
        for t in range(8):
            w.wait("vector", "tr", gate["xh_tr"][0])
            w.wait("vector", "tr2", gate["xh_tr"][1])
            ln_done = emit_ln(acc_h, xh[:], 1, src_off=t, dst_off=0)
            w.wait("tensor", "vc", ln_done)
            gate["xh_tr"] = emit_transposes(xh[:], xh2f_h, 1, dst_toff=t * 128)
            w.use_bank(4 + t % 2)
            rb = ps[4 + t % 2]
            w.wait("tensor", "tr", gate["xh_tr"][0])
            w.wait("tensor", "tr2", gate["xh_tr"][1])
            if t == 0 and half == 0:
                w.wait("tensor", "tr", rwl)
            for hc in range(8):
                mmr = w.run("tensor", lambda e, hc=hc, t=t, rb=rb, xh2f_h=xh2f_h: e.matmul(
                    rb[:, 0:8], xh2f_h[:, hc, ts_(t, 128)], rwB[:, hc: hc + 57: 8],
                    start=(hc == 0), stop=(hc == 7)), "tc" if hc == 7 else None)
            w.wait("vector", "tc", mmr)
            w.run("vector", lambda e, rb=rb: e.reduce_max(out=rscr[:, 0, 0:1], in_=rb[:, 0:8],
                                                          axis=AX.X, negate=True), "vc")
            vnm = w.cnt["vc"]
            w.wait("scalar", "vc", vnm)
            sex = w.run("scalar", lambda e, rb=rb: e.activation(out=rscr[:, 1, :], in_=rb[:, 0:8],
                                                                func=AF.Exp, bias=rscr[:, 0, 0:1],
                                                                scale=1.0), "sc")
            w.wait("vector", "sc", sex)
            w.run("vector", lambda e: e.reduce_sum(out=rscr[:, 0, 1:2], in_=rscr[:, 1, :], axis=AX.X), "vc")
            w.run("vector", lambda e: e.drain(), None)
            w.run("vector", lambda e: e.reciprocal(out=rscr[:, 0, 2:3], in_=rscr[:, 0, 1:2]), "vc")
            w.run("vector", lambda e: e.drain(), None)
            w.run("vector", lambda e: e.tensor_scalar(out=rscr[:, 1, :], in0=rscr[:, 1, :],
                                                      scalar1=rscr[:, 0, 2:3], scalar2=None,
                                                      op0=ALU.mult), "vc")
            w.run("vector", lambda e: e.drain(), None)
            w.run("vector", lambda e: e.reduce_max(out=rscr[:, 0, 3:4], in_=rscr[:, 1, :], axis=AX.X), "vc")
            w.run("vector", lambda e: e.drain(), None)
            w.run("vector", lambda e: e.tensor_scalar(out=rscr[:, 2, :], in0=rscr[:, 1, :],
                                                      scalar1=rscr[:, 0, 3:4], scalar2=None,
                                                      op0=ALU.is_ge), "vc")
            w.run("vector", lambda e: e.drain(), None)
            w.run("vector", lambda e: e.scalar_tensor_tensor(out=rscr[:, 2, :], in0=rscr[:, 2, :],
                                                             scalar=-2.0, in1=rscr[:, 1, :],
                                                             op0=ALU.mult, op1=ALU.add), "vc")
            w.run("vector", lambda e: e.drain(), None)
            w.run("vector", lambda e: e.reduce_max(out=rscr[:, 0, 4:5], in_=rscr[:, 2, :], axis=AX.X), "vc")
            w.run("vector", lambda e: e.drain(), None)
            w.run("vector", lambda e: e.tensor_scalar(out=rscr[:, 2, :], in0=rscr[:, 1, :],
                                                      scalar1=rscr[:, 0, 4:5], scalar2=None,
                                                      op0=ALU.is_ge), "vc")
            w.run("vector", lambda e: e.tensor_add(out=rscr[:, 0, 5:6], in0=rscr[:, 0, 3:4],
                                                   in1=rscr[:, 0, 4:5]), "vc")
            w.run("vector", lambda e: e.drain(), None)
            w.run("vector", lambda e: e.tensor_mul(out=rscr[:, 3, :], in0=rscr[:, 1, :],
                                                   in1=rscr[:, 2, :]), "vc")
            w.run("vector", lambda e: e.reciprocal(out=rscr[:, 0, 6:7], in_=rscr[:, 0, 5:6]), "vc")
            w.run("vector", lambda e: e.drain(), None)
            aff_vc[t] = w.run("vector", lambda e, t=t, aff_h=aff_h: e.tensor_scalar(
                out=aff_h[:, t, :], in0=rscr[:, 3, :], scalar1=rscr[:, 0, 6:7],
                scalar2=None, op0=ALU.mult), "vc")
            w.release_bank(4 + t % 2, "vc", aff_vc[t])
        if debug:
            w.wait("gpsimd", "vc", aff_vc[7])
            w.run("gpsimd", lambda e, half=half, aff_h=aff_h: e.dma_start(
                out=dbg["aff"][half], in_=aff_h), "dst", 16)
        return acc_h, xh2f_h, aff_h, lda

    def c_experts(half, acc_h, xh2f_h, aff_h, lda):
        gu_lds = {(half, 0, 0): gu_load(half, 0, 0), (half, 0, 1): gu_load(half, 0, 1)}
        wd_lds = {(half, 0): wd_load(half, 0)}
        for e_ in range(c_nexp):
            for ic in range(4):
                ld_cnt = gu_lds.pop((half, e_, ic), None)
                if ld_cnt is None:
                    ld_cnt = gu_load(half, e_, ic)
                w.wait("tensor", "wd", ld_cnt)
                slot = ic % 2
                for isub in range(4):
                    for tt in range(2):
                        j = (isub * 2 + tt) % 2
                        gb, ub = ps[0 + j], ps[2 + j]
                        w.use_bank(0 + j)
                        for hc in range(8):
                            mmg = w.run("tensor", lambda e, hc=hc, isub=isub, tt=tt, slot=slot, gb=gb,
                                        xh2f_h=xh2f_h: e.matmul(
                                gb, wgr[:, slot, hc, ts_(isub, 128)], xh2f_h[:, hc, ts_(tt, 512)],
                                start=(hc == 0), stop=(hc == 7)), "tc" if hc == 7 else None)
                        w.use_bank(2 + j)
                        for hc in range(8):
                            mmu = w.run("tensor", lambda e, hc=hc, isub=isub, tt=tt, slot=slot, ub=ub,
                                        xh2f_h=xh2f_h: e.matmul(
                                ub, wur[:, slot, hc, ts_(isub, 128)], xh2f_h[:, hc, ts_(tt, 512)],
                                start=(hc == 0), stop=(hc == 7)), "tc" if hc == 7 else None)
                        w.wait("scalar", "tc", mmg)
                        w.wait("scalar", "vc", gsil_gate.get(j, 0))
                        sg = w.run("scalar", lambda e, gb=gb, j=j: e.activation(
                            out=gsil[:, j, :], in_=gb, func=AF.Silu, bias=0.0, scale=1.0), "sc")
                        w.release_bank(0 + j, "sc", sg)
                        w.wait("vector", "tc", mmu)
                        w.wait("vector", "sc", sg)
                        vact = w.run("vector", lambda e, ub=ub, j=j, ic=ic, isub=isub, tt=tt: e.tensor_mul(
                            out=act[:, ic * 4 + isub, ts_(tt, 512)], in0=ub, in1=gsil[:, j, :]), "vc")
                        w.release_bank(2 + j, "vc", vact)
                        gsil_gate[j] = vact
                gu_mm_done[(half, e_, ic)] = w.cnt["tc"]
                nxt = (e_, ic + 2) if ic + 2 < 4 else (e_ + 1, ic - 2)
                if nxt[0] < NE and (half,) + nxt not in gu_lds:
                    gu_lds[(half,) + nxt] = gu_load(half, *nxt)
            act_vc = w.cnt["vc"]
            w.wait("tensor", "wd", wd_lds[(half, e_)])
            w.wait("tensor", "vc", act_vc)
            for t in range(8):
                for oh in range(2):
                    bid = 4 + (t * 2 + oh) % 4
                    bank = ps[bid]
                    w.use_bank(bid)
                    for icc in range(16):
                        mmd = w.run("tensor", lambda e, icc=icc, t=t, oh=oh, bank=bank: e.matmul(
                            bank, act[:, icc, ts_(t, 128)], wdr[:, icc, ts_(oh, 512)],
                            start=(icc == 0), stop=(icc == 15)), "tc" if icc == 15 else None)
                    w.wait("vector", "tc", mmd)
                    if e_ == 0:
                        w.wait("vector", "ld", lda)
                    vd = w.run("vector", lambda e, bank=bank, t=t, oh=oh, e_=e_, acc_h=acc_h,
                               aff_h=aff_h: e.scalar_tensor_tensor(
                        out=acc_h[:, t, ts_(oh, 512)], in0=bank, scalar=aff_h[:, t, e_: e_ + 1],
                        in1=acc_h[:, t, ts_(oh, 512)], op0=ALU.mult, op1=ALU.add), "vc")
                    w.release_bank(bid, "vc", vd)
            dn_mm_done[(half, e_)] = w.cnt["tc"]
            if e_ + 1 < NE and (half, e_ + 1) not in wd_lds:
                wd_lds[(half, e_ + 1)] = wd_load(half, e_ + 1)
        acc_free_vc = w.cnt["vc"]
        w.wait("sync", "vc", acc_free_vc)
        dsto = out[ts_(half, 1024), :].rearrange("(t p) h -> p t h", p=128)
        out_st.append(w.run("sync", lambda e, dsto=dsto, acc_h=acc_h: e.dma_start(
            out=dsto, in_=acc_h), "st", 16))

    p0 = c_prologue(0)
    p1 = c_prologue(1)
    c_experts(0, *p0)
    c_experts(1, *p1)

    if debug:
        w.wait("gpsimd", "vc", w.cnt["vc"])
        w.wait("gpsimd", "tr", w.cnt["tr"])
        for icc in range(16):
            w.run("gpsimd", lambda e, icc=icc: e.dma_start(out=dbg["act"][icc], in_=act[:, icc, :]), "dst", 16)
        for hc in range(8):
            w.run("gpsimd", lambda e, hc=hc: e.dma_start(out=dbg["xh2f"][hc], in_=xh2fT[:, 0, hc, :]), "dst", 16)

    if enddbg:
        w.wait("gpsimd", "st", w.cnt["st"])
        for hc in range(8):
            w.run("gpsimd", lambda e, hc=hc: e.dma_start(out=dbg["xh2f0"][hc], in_=xh2fT[:, 0, hc, :]), "dst", 16)
        w.run("gpsimd", lambda e: e.dma_start(out=dbg["aff0"], in_=affT[:, 0]), "dst", 16)
        w.wait("gpsimd", "dst", w.cnt["dst"])

    w.wait("sync", "st", w.cnt["st"])
    if debug:
        w.wait("gpsimd", "dst", w.cnt["dst"])
    return _replay(nc, w)


def _replay(nc, w):
    with nc.Block() as block:
        @block.sync
        def _(eng):
            for f in w.thunks["sync"]:
                f(eng)

        @block.gpsimd
        def _(eng):
            for f in w.thunks["gpsimd"]:
                f(eng)

        @block.vector
        def _(eng):
            for f in w.thunks["vector"]:
                f(eng)

        @block.scalar
        def _(eng):
            for f in w.thunks["scalar"]:
                f(eng)

        @block.tensor
        def _(eng):
            for f in w.thunks["tensor"]:
                f(eng)

    return nc


def make_masks(tiles):
    m = np.zeros((NQT, 8, 128, QTW), np.float32)
    for i, g in enumerate(tiles):
        nck = NCK[i]
        q = g * QTW + np.arange(QTW)[None, :]
        for j in range(8):
            kchunk = nck - 8 + j
            k = kchunk * 128 + np.arange(128)[:, None]
            m[i, j] = (k <= q).astype(np.float32)
    return m.astype(ml_dtypes.bfloat16)


def shard_inputs(inputs):
    x = np.ascontiguousarray(inputs["L0_attn_x"])
    maps = []
    for c in range(8):
        s, h = c // 2, c % 2
        tiles = TILESETS[h]
        x_seq = np.ascontiguousarray(np.asarray(x[s], np.float32))
        x_own = np.ascontiguousarray(
            np.concatenate([x_seq[ts_(g, QTW)] for g in tiles], axis=0))
        maps.append({
            "x_seq": x_seq, "x_own": x_own, "m_in": make_masks(tiles),
            "wq_in": inputs["L0_attn_Wq"], "wk_in": inputs["L0_attn_Wk"],
            "wv_in": inputs["L0_attn_Wv"], "wo_in": inputs["L0_attn_Wo"],
            "bq_in": inputs["L0_attn_bq"], "bk_in": inputs["L0_attn_bk"],
            "bv_in": inputs["L0_attn_bv"], "bo_in": inputs["L0_attn_bo"],
            "rw_in": inputs["L0_ffn_router_w"], "wg_in": inputs["L0_ffn_w_gate"],
            "wu_in": inputs["L0_ffn_w_up"], "wd_in": inputs["L0_ffn_w_down"],
        })
    return maps


def kernel(**inputs):
    inputs = {k: np.asarray(v) for k, v in inputs.items()}
    maps = shard_inputs(inputs)
    nc = build_nc(debug=False)
    res = run_bass_kernel_spmd(nc, maps, core_ids=list(range(8)))
    outf = np.zeros((4, SEQ, D), np.float32)
    for c in range(8):
        s, h = c // 2, c % 2
        o = res.results[c]["out"]
        for i, g in enumerate(TILESETS[h]):
            outf[s, ts_(g, QTW)] = o[ts_(i, QTW)]
    return outf


# version 20
# speedup vs baseline: 1.1666x; 1.0233x over previous
import os
import numpy as np
import ml_dtypes
from contextlib import ExitStack

import concourse.bass as bass
import concourse.mybir as mybir
from concourse.bass_utils import run_bass_kernel_spmd

F32 = mybir.dt.float32
BF16 = mybir.dt.bfloat16
AF = mybir.ActivationFunctionType
ALU = mybir.AluOpType
AX = mybir.AxisListType

NH, DKH, D, SEQ = 16, 64, 1024, 4096
NQT, QTW = 4, 512
OWN = NQT * QTW
NCK = [8, 16, 24, 32]
TILESETS = [[0, 3, 4, 7], [1, 2, 5, 6]]
ISQ = float(8.0 ** -0.5)
NE, DI = 8, 2048
EPS = 1e-5
_NOFILL = os.environ.get("K_NOFILL", "0") == "1"
_NOTR2 = True


def ts_(i, w):
    return slice(i * w, (i + 1) * w)


class Walker:

    def __init__(self, nc, stack):
        self.nc = nc
        self.stack = stack
        self.thunks = {"sync": [], "gpsimd": [], "vector": [], "scalar": [], "tensor": []}
        self.sems = {}
        self.cnt = {}
        self.bank_gate = {}

    def sem(self, name):
        self.sems[name] = self.stack.enter_context(self.nc.semaphore(name))
        self.cnt[name] = 0

    def wait(self, eng, sem, val):
        if val is None or val <= 0:
            return
        h = self.sems[sem]
        self.thunks[eng].append(lambda e, h=h, v=val: e.wait_ge(h, v))

    def run(self, eng, f, sem=None, k=1):
        h = self.sems[sem] if sem else None

        def thunk(e, f=f, h=h, k=k):
            inst = f(e)
            if h is not None:
                inst.then_inc(h, k)

        self.thunks[eng].append(thunk)
        if sem is not None:
            self.cnt[sem] += k
            return self.cnt[sem]
        return None

    def use_bank(self, bid, eng="tensor"):
        g = self.bank_gate.get(bid)
        if g:
            self.wait(eng, g[0], g[1])

    def release_bank(self, bid, sem, cnt):
        self.bank_gate[bid] = (sem, cnt)


def build_nc(debug=False, enddbg=False, phases="abc", c_halves=2, c_nexp=NE, n_tc=8, wprep=True, aparts="lntkvm"):
    nc = bass.Bass("TRN2", target_bir_lowering=False, debug=False,
                   detect_race_conditions=False)

    x_seq = nc.dram_tensor("x_seq", [SEQ, D], F32, kind="ExternalInput").ap()
    x_own = nc.dram_tensor("x_own", [OWN, D], F32, kind="ExternalInput").ap()
    m_in = nc.dram_tensor("m_in", [NQT, 8, 128, QTW], BF16, kind="ExternalInput").ap()
    wq_in = nc.dram_tensor("wq_in", [D, D], F32, kind="ExternalInput").ap()
    wk_in = nc.dram_tensor("wk_in", [D, D], F32, kind="ExternalInput").ap()
    wv_in = nc.dram_tensor("wv_in", [D, D], F32, kind="ExternalInput").ap()
    wo_in = nc.dram_tensor("wo_in", [D, D], F32, kind="ExternalInput").ap()
    bq_in = nc.dram_tensor("bq_in", [D], F32, kind="ExternalInput").ap()
    bk_in = nc.dram_tensor("bk_in", [D], F32, kind="ExternalInput").ap()
    bv_in = nc.dram_tensor("bv_in", [D], F32, kind="ExternalInput").ap()
    bo_in = nc.dram_tensor("bo_in", [D], F32, kind="ExternalInput").ap()
    rw_in = nc.dram_tensor("rw_in", [NE, D], F32, kind="ExternalInput").ap()
    wg_in = nc.dram_tensor("wg_in", [NE, D, DI], F32, kind="ExternalInput").ap()
    wu_in = nc.dram_tensor("wu_in", [NE, D, DI], F32, kind="ExternalInput").ap()
    wd_in = nc.dram_tensor("wd_in", [NE, DI, D], F32, kind="ExternalInput").ap()

    out = nc.dram_tensor("out", [OWN, D], F32, kind="ExternalOutput").ap()
    x1_dram = nc.dram_tensor("x1_scr", [OWN, D], F32, kind="Internal").ap()

    dbg = {}
    if enddbg:
        dbg["xh2f0"] = nc.dram_tensor("dbg_xh2f0", [8, 128, 1024], F32, kind="ExternalOutput").ap()
        dbg["aff0"] = nc.dram_tensor("dbg_aff0", [128, 8, 8], F32, kind="ExternalOutput").ap()
    if debug:
        dbg["k"] = nc.dram_tensor("dbg_k", [8, 128, SEQ], F32, kind="ExternalOutput").ap()
        dbg["v"] = nc.dram_tensor("dbg_v", [32, 128, NH * 65], F32, kind="ExternalOutput").ap()
        dbg["km2"] = nc.dram_tensor("dbg_km2", [16, 1], F32, kind="ExternalOutput").ap()
        dbg["x1"] = nc.dram_tensor("dbg_x1", [OWN, D], F32, kind="ExternalOutput").ap()
        dbg["aff"] = nc.dram_tensor("dbg_aff", [2, 128, 8, 8], F32, kind="ExternalOutput").ap()
        dbg["act"] = nc.dram_tensor("dbg_act", [16, 128, 1024], F32, kind="ExternalOutput").ap()
        dbg["xhf"] = nc.dram_tensor("dbg_xhf", [8, 128, 512], F32, kind="ExternalOutput").ap()
        dbg["xh"] = nc.dram_tensor("dbg_xh", [128, 2, 1024], F32, kind="ExternalOutput").ap()
        dbg["xh2f"] = nc.dram_tensor("dbg_xh2f", [8, 128, 1024], F32, kind="ExternalOutput").ap()

    stack = ExitStack()
    nc._walker_stack = stack

    w = Walker(nc, stack)
    for s in ("ld", "tr", "tr2", "wd", "st", "dst", "tc", "vc", "sc", "gc"):
        w.sem(s)

    en = stack.enter_context

    ones_all = en(nc.sbuf_tensor("ones_all", [16, 128], BF16))
    id16 = en(nc.sbuf_tensor("id16", [16, 16], BF16))
    id128f = en(nc.sbuf_tensor("id128f", [128, 128], F32))
    bqk_c = en(nc.sbuf_tensor("bqk_c", [128, 2, 8], F32))
    bn6 = en(nc.sbuf_tensor("bn6", [128, 12], F32))
    st4 = en(nc.sbuf_tensor("st4", [128, 4], F32))
    lnt = en(nc.sbuf_tensor("lnt", [128, 8], F32))
    nrmL = en(nc.sbuf_tensor("nrmL", [128, 8, 16], BF16))
    psA = en(nc.psum_tensor("psA", [128, 2, 512], F32))
    psB = en(nc.psum_tensor("psB", [128, 2, 512], F32))
    psD = en(nc.psum_tensor("psD", [128, 2, 512], F32))
    psC = en(nc.psum_tensor("psC", [128, 2, 512], F32))
    pairs3 = [psA[:], psB[:], psC[:]]
    ps = [psA[:, 0, :], psA[:, 1, :], psB[:, 0, :], psB[:, 1, :],
          psD[:, 0, :], psD[:, 1, :], psC[:, 0, :], psC[:, 1, :]]

    stackAB = ExitStack()
    enAB = stackAB.enter_context
    xo = en(nc.sbuf_tensor("xo", [128, 3, D], F32))
    idt128 = xo[:, 0, 0:128]
    xhf = enAB(nc.sbuf_tensor("xhf", [128, 8, QTW], BF16))
    ktens = enAB(nc.sbuf_tensor("ktens", [128, 8, SEQ], BF16))
    vtens = enAB(nc.sbuf_tensor("vtens", [128, 32, NH * 65], BF16))
    km2 = enAB(nc.sbuf_tensor("km2", [16, 1], F32))
    cm2 = enAB(nc.sbuf_tensor("cm2", [16, 1], F32))
    ksq = enAB(nc.sbuf_tensor("ksq", [128, 2, 512], BF16))
    idtmp = ksq[0:16, 1, 0:16]
    cmdiag = ksq[0:16, 0, 0:16]
    vpad = vtens[:].rearrange("p c (h e) -> p c h e", e=65)

    for j, b in enumerate((bq_in, bk_in)):
        src = b.rearrange("(oc p) -> p oc", p=128)
        w.run("sync", lambda e, j=j, src=src: e.dma_start(out=bqk_c[:, j, :], in_=src,
                                                          allow_slow_non_contiguous=True), "ld", 16)
    bld = w.cnt["ld"]
    w.run("gpsimd", lambda e: e.iota(idtmp, pattern=[[1, 16]], base=0, channel_multiplier=-1,
                                     allow_small_or_imprecise_dtypes=True), "gc")
    w.run("gpsimd", lambda e: e.iota(idt128, pattern=[[1, 128]], base=0, channel_multiplier=-1,
                                     allow_small_or_imprecise_dtypes=True), "gc")
    iota_done = w.cnt["gc"]

    w.run("vector", lambda e: e.memset(ones_all[:], 1.0), "vc")
    w.run("vector", lambda e: e.memset(km2[:], 0.0), "vc")
    w.wait("vector", "ld", bld)
    for j in range(2):
        w.run("vector", lambda e, j=j: e.tensor_scalar(out=bqk_c[:, j, :], in0=bqk_c[:, j, :],
                                                       scalar1=ISQ, scalar2=None, op0=ALU.mult), "vc")
    w.wait("vector", "gc", iota_done)
    w.run("vector", lambda e: e.tensor_single_scalar(out=id16[:], in_=idtmp, scalar=0.0,
                                                     op=ALU.is_equal), "vc")
    w.run("vector", lambda e: e.tensor_single_scalar(out=id128f[:], in_=idt128, scalar=0.0,
                                                     op=ALU.is_equal), "vc")
    w.run("vector", lambda e: e.memset(nrmL[:], 0.0), "vc")
    for oc in range(8):
        w.run("vector", lambda e, oc=oc: e.memset(nrmL[0:64, oc, 2 * oc: 2 * oc + 1], 1.0), "vc")
        w.run("vector", lambda e, oc=oc: e.memset(nrmL[64:128, oc, 2 * oc + 1: 2 * oc + 2], 1.0), "vc")
    setup_v = w.cnt["vc"]


    def emit_ln(src_ap, dst_ap, n_ts, src_off=0, dst_off=0):
        for t in range(n_ts):
            si, di = src_off + t, dst_off + t
            w.run("vector", lambda e, si=si: e.bn_stats(out=bn6[:, 0:6], in_=src_ap[:, si, 0:512]), "vc")
            w.run("vector", lambda e, si=si: e.bn_stats(out=bn6[:, 6:12], in_=src_ap[:, si, 512:1024]), "vc")
            w.run("vector", lambda e: e.bn_aggr(out=st4[:, 0:2], in_=bn6[:, 0:6]), "vc")
            w.run("vector", lambda e: e.bn_aggr(out=st4[:, 2:4], in_=bn6[:, 6:12]), "vc")
            w.run("vector", lambda e: e.drain(), None)
            w.run("vector", lambda e: e.tensor_scalar(out=lnt[:, 1:2], in0=st4[:, 0:1],
                                                      scalar1=st4[:, 2:3], scalar2=0.5,
                                                      op0=ALU.add, op1=ALU.mult), "vc")
            w.run("vector", lambda e: e.tensor_sub(out=lnt[:, 2:3], in0=st4[:, 0:1], in1=st4[:, 2:3]), "vc")
            w.run("vector", lambda e: e.tensor_scalar(out=lnt[:, 5:6], in0=st4[:, 1:2],
                                                      scalar1=st4[:, 3:4], scalar2=0.5,
                                                      op0=ALU.add, op1=ALU.mult), "vc")
            w.run("vector", lambda e: e.drain(), None)
            w.run("vector", lambda e: e.tensor_scalar(out=lnt[:, 3:4], in0=lnt[:, 2:3],
                                                      scalar1=lnt[:, 2:3], scalar2=4.0 * EPS,
                                                      op0=ALU.mult, op1=ALU.add), "vc")
            w.run("vector", lambda e: e.drain(), None)
            vvar = w.run("vector", lambda e: e.tensor_scalar(out=lnt[:, 6:7], in0=lnt[:, 3:4], scalar1=0.25,
                                                             scalar2=lnt[:, 5:6], op0=ALU.mult, op1=ALU.add), "vc")
            w.wait("scalar", "vc", vvar)
            w.run("scalar", lambda e: e.activation(out=lnt[:, 7:8], in_=lnt[:, 6:7], func=AF.Ln,
                                                   bias=0.0, scale=1.0), "sc")
            w.run("scalar", lambda e: e.drain(), None)
            sr = w.run("scalar", lambda e: e.activation(out=lnt[:, 6:7], in_=lnt[:, 7:8], func=AF.Exp,
                                                        bias=0.0, scale=-0.5), "sc")
            w.wait("vector", "sc", sr)
            w.run("vector", lambda e, si=si, di=di: e.tensor_scalar(
                out=dst_ap[:, di, :], in0=src_ap[:, si, :],
                scalar1=lnt[:, 1:2], scalar2=lnt[:, 6:7],
                op0=ALU.subtract, op1=ALU.mult), "vc")
        return w.cnt["vc"]

    def emit_transposes(src_ap, dst_ap, n_ts, src_off=0, dst_toff=0, banks=(5, 6, 7)):
        for hc in range(8):
            bid = banks[hc % len(banks)]
            bank = ps[bid]
            w.use_bank(bid)
            for t in range(n_ts):
                mmt = w.run("tensor", lambda e, t=t, hc=hc, bank=bank: e.transpose(
                    bank[:, ts_(t, 128)], src_ap[:, t, ts_(hc, 128)], id128f[:]),
                    "tc" if t == n_ts - 1 else None)
            w.wait("scalar", "tc", mmt)
            dst = dst_ap[:, hc, dst_toff + src_off * 128: dst_toff + (src_off + n_ts) * 128]
            w.run("scalar", lambda e, dst=dst, bank=bank, n_ts=n_ts: e.copy(
                out=dst, in_=bank[:, 0: n_ts * 128]), None)
            ev = w.run("scalar", lambda e: e.drain(), "tr")
            w.release_bank(bid, "tr", ev)
        return (w.cnt["tr"], w.cnt["tr2"])

    def emit_weight_prep(w_hbm, wtmp, wT):
        for g in range(4):
            src = w_hbm[ts_(g, 256), :].rearrange("(oc p) h -> p oc h", p=128)
            w.wait("sync", "tr", w.cnt["tr"])
            dcnt = w.run("sync", lambda e, src=src: e.dma_start(out=wtmp[:], in_=src), "ld", 16)
            w.wait("tensor", "ld", dcnt)
            for hc in range(8):
                bid = (6, 7)[hc % 2]
                bank = ps[bid]
                w.use_bank(bid)
                for oc in range(2):
                    mmt = w.run("tensor", lambda e, hc=hc, oc=oc, bank=bank: e.transpose(
                        bank[:, ts_(oc, 128)], wtmp[:, oc, ts_(hc, 128)], id128f[:]),
                        "tc" if oc == 1 else None)
                w.wait("scalar", "tc", mmt)
                dst = wT[:, hc, g * 256: (g + 1) * 256]
                ev = w.run("scalar", lambda e, dst=dst, bank=bank: e.copy(
                    out=dst, in_=bank[:, 0:256]), "tr")
                w.release_bank(bid, "tr", ev)
        return (w.cnt["tr"], w.cnt["tr2"])

    gate = {"xo_vc": setup_v, "xh_tr": (0, 0), "xhf_tc": 0, "xo_st": 0, "xo_slot": 2, "xo_free": [0, 0, 0]}

    def emit_x_tile(rows_ap, dst=None, xhf_gate_tc=None):
        dst = xhf if dst is None else dst
        gtc = gate["xhf_tc"] if xhf_gate_tc is None else xhf_gate_tc
        for u in range(4):
            slot = gate["xo_slot"] = (gate["xo_slot"] + 1) % 3
            src = rows_ap[ts_(u, 128), :].rearrange("(t p) h -> p t h", p=128)
            w.wait("sync", "tr", gate["xo_free"][slot])
            w.wait("sync", "st", gate["xo_st"])
            ldx = w.run("sync", lambda e, src=src, slot=slot: e.dma_start(
                out=xo[:, slot, :], in_=src), "ld", 16)
            w.wait("vector", "ld", ldx)
            ln_done = emit_ln(xo[:, slot: slot + 1, :], xo[:, slot: slot + 1, :], 1)
            gate["xo_vc"] = ln_done
            if u == 0:
                w.wait("tensor", "tc", gtc)
            w.wait("tensor", "vc", ln_done)
            tr = emit_transposes(xo[:, slot: slot + 1, :], dst, 1, dst_toff=u * 128)
            gate["xh_tr"] = tr
            gate["xo_free"][slot] = tr[0]
        return gate["xh_tr"]

    stackA = ExitStack()
    enA = stackA.enter_context
    wkT = enA(nc.sbuf_tensor("wkT", [128, 8, D], BF16))
    wvT = enA(nc.sbuf_tensor("wvT", [128, 8, D], BF16))
    wtmpA = enA(nc.sbuf_tensor("wtmpA", [128, 2, D], F32))
    bv_row = enA(nc.sbuf_tensor("bv_row", [1, D], BF16))
    xhf2 = enA(nc.sbuf_tensor("xhf2", [128, 8, QTW], BF16))

    w.run("gpsimd", lambda e: e.dma_start(out=bv_row[:], in_=bv_in.rearrange("(o h) -> o h", o=1)), "wd", 16)
    if wprep:
        emit_weight_prep(wk_in, wtmpA, wkT)
        trv, trv2 = emit_weight_prep(wv_in, wtmpA, wvT)
    else:
        trv, trv2 = w.cnt["tr"], w.cnt["tr2"]

    vones = w.run("vector", lambda e: e.memset(vpad[:, :, :, 64:65], 1.0), "vc")

    w.wait("sync", "vc", setup_v)
    w.wait("tensor", "tr", trv)
    w.wait("tensor", "tr2", trv2)
    w.wait("tensor", "vc", setup_v)
    w.wait("scalar", "vc", setup_v)
    wd_bias = w.cnt["wd"]

    xhf_mm_done = {0: 0, 1: 0}
    for tc in range(n_tc):
        buf = xhf if tc % 2 == 0 else xhf2
        if "l" in aparts:
            emit_x_tile(x_seq[ts_(tc, 512), :], dst=buf, xhf_gate_tc=xhf_mm_done[tc % 2])
        w.wait("tensor", "tr", gate["xh_tr"][0])
        w.wait("tensor", "tr2", gate["xh_tr"][1])

        kev_sc = {}
        if "k" not in aparts:
            continue
        for oc in range(8):
            bid = oc % 2
            bank = ps[bid]
            w.use_bank(bid)
            for hc in range(8):
                mmk = w.run("tensor", lambda e, hc=hc, oc=oc, bank=bank, buf=buf: e.matmul(
                    bank, wkT[:, hc, ts_(oc, 128)], buf[:, hc, :],
                    start=(hc == 0), stop=(hc == 7)), "tc" if hc == 7 else None)
            w.wait("scalar", "tc", mmk)
            kev_sc[oc] = w.run("scalar", lambda e, tc=tc, oc=oc, bank=bank: e.activation(
                out=ktens[:, oc, ts_(tc, 512)], in_=bank, func=AF.Identity,
                bias=bqk_c[:, 1, oc: oc + 1], scale=ISQ), "sc")
            w.release_bank(bid, "sc", kev_sc[oc])

        if "v" not in aparts:
            gate["xhf_tc"] = w.cnt["tc"]
            continue
        w.wait("tensor", "vc", vones)
        for t in range(4):
            for oh in range(2):
                bid = 2 + (t * 2 + oh) % 2
                bank = ps[bid]
                w.use_bank(bid)
                for hc in range(8):
                    w.run("tensor", lambda e, hc=hc, t=t, oh=oh, bank=bank, buf=buf: e.matmul(
                        bank, buf[:, hc, ts_(t, 128)], wvT[:, hc, ts_(oh, 512)],
                        start=(hc == 0), stop=False), None)
                mmv = w.run("tensor", lambda e, oh=oh, bank=bank: e.matmul(
                    bank, ones_all[0:1, 0:128], bv_row[0:1, ts_(oh, 512)],
                    start=False, stop=True), "tc")
                w.wait("vector", "tc", mmv)
                kc = tc * 4 + t
                dst = vpad[:, kc, ts_(oh, 8), 0:64]
                srcp = bank.rearrange("p (h d) -> p h d", d=64)
                vcnt = w.run("vector", lambda e, dst=dst, srcp=srcp: e.tensor_copy(
                    out=dst, in_=srcp), "vc")
                w.release_bank(bid, "vc", vcnt)
        xhf_mm_done[tc % 2] = w.cnt["tc"]
        gate["xhf_tc"] = w.cnt["tc"]

        if "m" not in aparts:
            continue
        w.use_bank(4)
        nrm_mm = {}
        for oc in range(8):
            sq = ksq[:, oc % 2, :]
            w.wait("vector", "sc", kev_sc[oc])
            if oc >= 2:
                w.wait("vector", "tc", nrm_mm[oc - 2])
            vsq = w.run("vector", lambda e, sq=sq, oc=oc, tc=tc: e.tensor_mul(
                out=sq, in0=ktens[:, oc, ts_(tc, 512)], in1=ktens[:, oc, ts_(tc, 512)]), "vc")
            w.wait("tensor", "vc", vsq)
            nrm_mm[oc] = w.run("tensor", lambda e, sq=sq, oc=oc: e.matmul(
                ps[4][0:16, :], nrmL[:, oc, :], sq,
                start=(oc == 0), stop=(oc == 7)), "tc")
        w.wait("vector", "tc", nrm_mm[7])
        w.run("vector", lambda e: e.reduce_max(out=lnt[0:16, 0:1], in_=ps[4][0:16, :], axis=AX.X), "vc")
        w.run("vector", lambda e: e.drain(), None)
        nm = w.run("vector", lambda e: e.tensor_max(out=km2[:], in0=km2[:], in1=lnt[0:16, 0:1]), "vc")
        w.release_bank(4, "vc", nm)

    phaseA_vc, phaseA_sc = w.cnt["vc"], w.cnt["sc"]

    if debug:
        w.wait("gpsimd", "sc", phaseA_sc)
        w.wait("gpsimd", "vc", phaseA_vc)
        w.wait("gpsimd", "tr", w.cnt["tr"])
        for hc in range(8):
            w.run("gpsimd", lambda e, hc=hc: e.dma_start(out=dbg["xhf"][hc], in_=xhf[:, hc, :]), "dst", 16)
        w.run("gpsimd", lambda e: e.dma_start(out=dbg["xh"], in_=xh[:]), "dst", 16)
        for oc in range(8):
            w.run("gpsimd", lambda e, oc=oc: e.dma_start(out=dbg["k"][oc], in_=ktens[:, oc, :]), "dst", 16)
        for kc in range(32):
            w.run("gpsimd", lambda e, kc=kc: e.dma_start(out=dbg["v"][kc], in_=vtens[:, kc, :]), "dst", 16)
        w.run("gpsimd", lambda e: e.dma_start(out=dbg["km2"], in_=km2[:]), "dst", 16)

    if "b" not in phases:
        if debug:
            w.wait("sync", "st", w.cnt["st"])
            w.wait("gpsimd", "dst", w.cnt["dst"])
        return _replay(nc, w)
    stackA.close()
    stackB = ExitStack()
    enB = stackB.enter_context
    stackB2 = ExitStack()
    wqT = enB(nc.sbuf_tensor("wqT", [128, 8, D], BF16))
    woT = enB(nc.sbuf_tensor("woT", [128, 8, D], BF16))
    bo_row = enB(nc.sbuf_tensor("bo_row", [1, D], BF16))
    wtmpB = stackB2.enter_context(nc.sbuf_tensor("wtmpB", [128, 2, D], F32))

    w.wait("gpsimd", "tc", gate["xhf_tc"])
    w.wait("sync", "tc", gate["xhf_tc"])
    w.run("gpsimd", lambda e: e.dma_start(out=bo_row[:], in_=bo_in.rearrange("(o h) -> o h", o=1)), "wd", 16)
    emit_weight_prep(wq_in, wtmpB, wqT)
    trq, trq2 = emit_weight_prep(wo_in, wtmpB, woT)
    stackB2.close()

    qa = enB(nc.sbuf_tensor("qa", [128, 8, QTW], BF16))
    ptring = enB(nc.sbuf_tensor("ptring", [128, 4, QTW], BF16))
    praw = xhf[0:1, 7, :]
    lbc_sb = enB(nc.sbuf_tensor("lbc_sb", [64, QTW], BF16))
    cnegb = enB(nc.sbuf_tensor("cnegb", [128, 16], F32))
    mslot = enB(nc.sbuf_tensor("mslot", [128, 8, QTW], BF16))

    w.wait("tensor", "tr", trq)
    w.wait("tensor", "tr2", trq2)
    mask_use_vc = 0
    tile_av_end = 0
    last_mask_vc = 0
    prev_lbc_mm = 0
    phaseA_end_tc = gate["xhf_tc"]

    for i in range(NQT):
        nck = NCK[i]
        if i == 0:
            w.wait("sync", "tc", phaseA_end_tc)
        w.wait("sync", "vc", mask_use_vc)
        msrc = m_in[i].rearrange("c p j -> p c j")
        mld = w.run("sync", lambda e, msrc=msrc: e.dma_start(out=mslot[:], in_=msrc), "ld", 16)

        emit_x_tile(x_own[ts_(i, QTW), :])
        w.wait("tensor", "tr", gate["xh_tr"][0])
        w.wait("tensor", "tr2", gate["xh_tr"][1])

        qev_sc = {}
        for oc in range(8):
            bid = oc % 2
            bank = ps[bid]
            w.use_bank(bid)
            for hc in range(8):
                mmq = w.run("tensor", lambda e, hc=hc, oc=oc, bank=bank: e.matmul(
                    bank, wqT[:, hc, ts_(oc, 128)], xhf[:, hc, :],
                    start=(hc == 0), stop=(hc == 7)), "tc" if hc == 7 else None)
            w.wait("scalar", "tc", mmq)
            qev_sc[oc] = w.run("scalar", lambda e, oc=oc, bank=bank: e.activation(
                out=qa[:, oc, :], in_=bank, func=AF.Identity,
                bias=bqk_c[:, 0, oc: oc + 1], scale=ISQ), "sc")
            w.release_bank(bid, "sc", qev_sc[oc])
        gate["xhf_tc"] = w.cnt["tc"]

        w.use_bank(4)
        nrm_mm = {}
        for oc in range(8):
            sq = ksq[:, oc % 2, :]
            w.wait("vector", "sc", qev_sc[oc])
            if oc < 2:
                w.wait("vector", "tc", tile_av_end)
            if oc >= 2:
                w.wait("vector", "tc", nrm_mm[oc - 2])
            vsq = w.run("vector", lambda e, sq=sq, oc=oc: e.tensor_mul(
                out=sq, in0=qa[:, oc, :], in1=qa[:, oc, :]), "vc")
            w.wait("tensor", "vc", vsq)
            nrm_mm[oc] = w.run("tensor", lambda e, sq=sq, oc=oc: e.matmul(
                ps[4][0:16, :], nrmL[:, oc, :], sq,
                start=(oc == 0), stop=(oc == 7)), "tc")
        w.wait("vector", "tc", nrm_mm[7])
        w.run("vector", lambda e: e.reduce_max(out=lnt[0:16, 0:1], in_=ps[4][0:16, :], axis=AX.X), "vc")
        w.run("vector", lambda e: e.drain(), None)
        vq = w.run("vector", lambda e: e.tensor_max(out=cm2[:], in0=km2[:], in1=lnt[0:16, 0:1]), "vc")
        w.release_bank(4, "vc", vq)
        w.run("vector", lambda e: e.drain(), None)
        vcd = w.run("vector", lambda e: e.tensor_scalar(out=cmdiag, in0=id16[:],
                                                        scalar1=cm2[0:16, 0:1], scalar2=-1.0,
                                                        op0=ALU.mult, op1=ALU.mult), "vc")
        w.use_bank(5)
        w.wait("tensor", "vc", vcd)
        mmc = w.run("tensor", lambda e: e.matmul(ps[5][:, 0:16], ones_all[0:16, 0:128], cmdiag,
                                                 start=True, stop=True), "tc")
        w.wait("scalar", "tc", mmc)
        scb = w.run("scalar", lambda e: e.copy(out=cnegb[:], in_=ps[5][:, 0:16]), "sc")
        w.release_bank(5, "sc", scb)
        w.run("scalar", lambda e: e.drain(), None)

        w.wait("scalar", "ld", mld)
        w.wait("vector", "ld", mld)
        w.wait("scalar", "sc", scb)
        pair_banks = {0: (0, 1), 1: (2, 3), 2: (6, 7)}
        pt_slots = {0: ptring[:, 0:2, :], 1: ptring[:, 2:4, :], 2: ksq[:, 0:2, :]}
        for hp in range(NH // 2):
            oc = hp
            ngrp = nck // 2
            w.wait("tensor", "sc", qev_sc[oc])
            w.use_bank(4)
            w.use_bank(5)
            st_done = {}
            pt_prod = {}
            pair_last_av = {}

            def st_step(g):
                pa, pb = (2 * g) % 3, (2 * g + 1) % 3
                for pr, j in ((0, 0), (1, 0), (0, 1), (1, 1)):
                    pair = pa if pr == 0 else pb
                    bid = pair_banks[pair][j]
                    c = 2 * g + j
                    w.use_bank(bid)
                    st_done[(g, pr, j)] = w.run(
                        "tensor", lambda e, c=c, oc=oc, pr=pr, bid=bid: e.matmul(
                            ps[bid], ktens[64 * pr:64 * (pr + 1), oc, ts_(c, 128)],
                            qa[64 * pr:64 * (pr + 1), oc, :],
                            start=True, stop=True, tile_position=(64 * pr, 0)), "tc")

            def exp_step(g):
                nonlocal last_mask_vc
                pa, pb = (2 * g) % 3, (2 * g + 1) % 3
                masked = 2 * g >= nck - 8
                for pr, pair in ((0, pa), (1, pb)):
                    hd = 2 * hp + pr
                    slots = pt_slots[pair]
                    w.wait("scalar", "tc", st_done[(g, pr, 1)])
                    w.wait("scalar", "tc", pair_last_av.get(pair, 0))
                    se = w.run("scalar", lambda e, pair=pair, hd=hd, slots=slots: e.activation(
                        out=slots, in_=pairs3[pair], func=AF.Exp,
                        bias=cnegb[:, hd: hd + 1], scale=1.0), "sc")
                    if masked:
                        w.wait("vector", "sc", se)
                        for j in (0, 1):
                            c = 2 * g + j
                            last_mask_vc = w.run(
                                "vector", lambda e, c=c, j=j, slots=slots, nck=nck: e.tensor_mul(
                                    out=slots[:, j, :], in0=slots[:, j, :],
                                    in1=mslot[:, c - (nck - 8), :]), "vc")
                        pt_prod[(g, pr)] = ("vc", last_mask_vc)
                    else:
                        pt_prod[(g, pr)] = ("sc", se)
                    for bid in pair_banks[pair]:
                        w.release_bank(bid, *pt_prod[(g, pr)])

            def av_step(g):
                pa, pb = (2 * g) % 3, (2 * g + 1) % 3
                for pr, pair in ((0, pa), (1, pb)):
                    hd = 2 * hp + pr
                    slots = pt_slots[pair]
                    sem_p, cnt_p = pt_prod[(g, pr)]
                    w.wait("tensor", sem_p, cnt_p)
                    ov = ps[4 + pr]
                    for j in (0, 1):
                        c = 2 * g + j
                        cnt = w.run("tensor", lambda e, c=c, j=j, slots=slots, hd=hd, ov=ov,
                                    nck=nck: e.matmul(
                            ov[0:65, :], vtens[:, c, 65 * hd: 65 * hd + 65],
                            slots[:, j, :],
                            start=(c == 0), stop=(c == nck - 1)), "tc")
                    pair_last_av[pair] = cnt

            for g in range(ngrp):
                st_step(g)
                exp_step(g)
                if g >= 1:
                    av_step(g - 1)
            av_step(ngrp - 1)

            prev_va = None
            for pr in (0, 1):
                hd = 2 * hp + pr
                ov = ps[4 + pr]
                w.wait("vector", "tc", pair_last_av[(2 * (ngrp - 1) + pr) % 3])
                w.wait("vector", "tc", prev_lbc_mm)
                w.wait("vector", "sc", w.cnt["sc"])

                def _vrec(e, ov=ov):
                    with nc.allow_low_precision(reason="softmax 1/l as bf16 matmul operand"):
                        return e.reciprocal(out=praw, in_=ov[64:65, :])
                vrec = w.run("vector", _vrec, "vc")
                last_mask_vc = vrec
                w.wait("tensor", "vc", vrec)
                prev_lbc_mm = w.run("tensor", lambda e, ov=ov: e.matmul(
                    ov[64:128, :], ones_all[0:1, 0:64], praw, start=True, stop=True,
                    tile_position=(0, 64)), "tc")
                w.wait("scalar", "tc", prev_lbc_mm)
                slbc = w.run("scalar", lambda e, ov=ov: e.copy(out=lbc_sb[:], in_=ov[64:128, :]), "sc")
                w.wait("vector", "sc", slbc)
                va = w.run("vector", lambda e, ov=ov, pr=pr, oc=oc: e.tensor_mul(
                    out=qa[64 * pr:64 * (pr + 1), oc, :], in0=ov[0:64, :], in1=lbc_sb[:]), "vc")
                w.release_bank(4 + pr, "vc", va)
        mask_use_vc = last_mask_vc
        tile_av_end = w.cnt["tc"]

        a_done_vc = w.cnt["vc"]
        for t in range(4):
            slot = gate["xo_slot"] = (gate["xo_slot"] + 1) % 3
            src = x_own[ts_(i * 4 + t, 128), :].rearrange("(a p) h -> p a h", p=128)
            w.wait("sync", "tr", gate["xo_free"][slot])
            w.wait("sync", "st", gate["xo_st"])
            ldx = w.run("sync", lambda e, src=src, slot=slot: e.dma_start(
                out=xo[:, slot, :], in_=src), "ld", 16)
            for oh in range(2):
                bid = 4 + oh
                bank = ps[bid]
                w.use_bank(bid)
                if t == 0 and oh == 0:
                    w.wait("tensor", "vc", a_done_vc)
                for oc in range(8):
                    w.run("tensor", lambda e, oc=oc, t=t, oh=oh, bank=bank: e.matmul(
                        bank, qa[:, oc, ts_(t, 128)], woT[:, oc, ts_(oh, 512)],
                        start=(oc == 0), stop=False), None)
                mmo = w.run("tensor", lambda e, oh=oh, bank=bank: e.matmul(
                    bank, ones_all[0:1, 0:128], bo_row[0:1, ts_(oh, 512)],
                    start=False, stop=True), "tc")
                w.wait("vector", "tc", mmo)
                w.wait("vector", "ld", ldx)
                xslice = xo[:, slot, ts_(oh, 512)]
                vx1 = w.run("vector", lambda e, bank=bank, xslice=xslice: e.tensor_add(
                    out=xslice, in0=bank, in1=xslice), "vc")
                w.release_bank(bid, "vc", vx1)
                w.wait("sync", "vc", vx1)
                row0 = i * QTW + t * 128
                dst = x1_dram[row0: row0 + 128, ts_(oh, 512)]
                w.run("sync", lambda e, dst=dst, xslice=xslice: e.dma_start(
                    out=dst, in_=xslice), "st", 16)
                if debug:
                    ddst = dbg["x1"][row0: row0 + 128, ts_(oh, 512)]
                    w.run("sync", lambda e, ddst=ddst, xslice=xslice: e.dma_start(
                        out=ddst, in_=xslice), "st", 16)
            gate["xo_vc"] = w.cnt["vc"]
            gate["xo_st"] = w.cnt["st"]
    x1_all_st = w.cnt["st"]

    if "c" not in phases:
        if debug:
            w.wait("sync", "st", w.cnt["st"])
            w.wait("gpsimd", "dst", w.cnt["dst"])
        return _replay(nc, w)
    stackB.close()
    stackAB.close()
    stackC = ExitStack()
    enC = stackC.enter_context
    accT = enC(nc.sbuf_tensor("accT", [128, 2, 8, D], F32))
    xh2fT = enC(nc.sbuf_tensor("xh2fT", [128, 2, 8, 1024], BF16))
    act = enC(nc.sbuf_tensor("act", [128, 16, 1024], BF16))
    wgr = enC(nc.sbuf_tensor("wgr", [128, 2, 8, 512], BF16))
    wur = enC(nc.sbuf_tensor("wur", [128, 2, 8, 512], BF16))
    wdr = enC(nc.sbuf_tensor("wdr", [128, 16, D], BF16))
    gsil = enC(nc.sbuf_tensor("gsil", [128, 1, 512], BF16))
    affT = enC(nc.sbuf_tensor("affT", [128, 2, 8, 8], F32))
    rwB = enC(nc.sbuf_tensor("rwB", [128, 64], BF16))
    rw_stage = accT[0:64, 0, 0, 0:128]
    rscr = enC(nc.sbuf_tensor("rscr", [128, 4, 8], F32))

    w.wait("gpsimd", "st", x1_all_st)

    w.wait("sync", "st", x1_all_st)
    rwsrc = rw_in.rearrange("e (a j) -> (e a) j", j=128)
    rwld = w.run("sync", lambda e: e.dma_start(out=rw_stage, in_=rwsrc), "ld", 16)
    w.use_bank(6)
    w.wait("tensor", "ld", rwld)
    mmrw = w.run("tensor", lambda e: e.transpose(ps[6][:, 0:64], rw_stage, id128f[0:64, 0:64]), "tc")
    w.wait("scalar", "tc", mmrw)
    w.run("scalar", lambda e: e.copy(out=rwB[:], in_=ps[6][:, 0:64]), None)
    rwl = w.run("scalar", lambda e: e.drain(), "tr")
    w.release_bank(6, "tr", rwl)

    gu_mm_done = {}
    dn_mm_done = {}

    def gu_load(half, e_, ic):
        slot = ic % 2
        prev = gu_mm_done.get((half, e_, ic - 2)) or gu_mm_done.get((half, e_ - 1, ic + 2)) \
            or gu_mm_done.get((half - 1, NE - 1, ic + 2))
        if prev:
            w.wait("gpsimd", "tc", prev)
        for t_hbm, ring in ((wg_in, wgr), (wu_in, wur)):
            src = t_hbm[e_, :, ts_(ic, 512)].rearrange("(hc p) i -> p hc i", p=128)
            w.run("gpsimd", lambda e, src=src, ring=ring, slot=slot: e.dma_start(
                out=ring[:, slot, :, :], in_=src), "wd", 16)
        return w.cnt["wd"]

    def wd_load(half, e_):
        prev = dn_mm_done.get((half, e_ - 1)) or dn_mm_done.get((half - 1, e_ + NE - 1))
        if prev:
            w.wait("gpsimd", "tc", prev)
        src = wd_in[e_].rearrange("(ic p) h -> p ic h", p=128)
        w.run("gpsimd", lambda e, src=src: e.dma_start(out=wdr[:], in_=src), "wd", 16)
        return w.cnt["wd"]

    out_st = []
    gsil_gate = {}

    def c_prologue(half):
        acc_h = accT[:, half]
        xh2f_h = xh2fT[:, half]
        aff_h = affT[:, half]
        w.wait("sync", "st", x1_all_st)
        w.wait("sync", "tc", mmrw)
        asrc = x1_dram[ts_(half, 1024), :].rearrange("(t p) h -> p t h", p=128)
        lda = w.run("sync", lambda e, asrc=asrc, acc_h=acc_h: e.dma_start(out=acc_h, in_=asrc), "ld", 16)
        w.wait("vector", "ld", lda)
        aff_vc = {}
        for t in range(8):
            slot = gate["xo_slot"] = (gate["xo_slot"] + 1) % 3
            w.wait("vector", "tr", gate["xo_free"][slot])
            w.wait("vector", "st", gate["xo_st"])
            ln_done = emit_ln(acc_h, xo[:, slot: slot + 1, :], 1, src_off=t, dst_off=0)
            w.wait("tensor", "vc", ln_done)
            tr = emit_transposes(xo[:, slot: slot + 1, :], xh2f_h, 1, dst_toff=t * 128)
            gate["xh_tr"] = tr
            gate["xo_free"][slot] = tr[0]
            w.use_bank(4 + t % 2)
            rb = ps[4 + t % 2]
            w.wait("tensor", "tr", gate["xh_tr"][0])
            w.wait("tensor", "tr2", gate["xh_tr"][1])
            if t == 0 and half == 0:
                w.wait("tensor", "tr", rwl)
            for hc in range(8):
                mmr = w.run("tensor", lambda e, hc=hc, t=t, rb=rb, xh2f_h=xh2f_h: e.matmul(
                    rb[:, 0:8], xh2f_h[:, hc, ts_(t, 128)], rwB[:, hc: hc + 57: 8],
                    start=(hc == 0), stop=(hc == 7)), "tc" if hc == 7 else None)
            w.wait("vector", "tc", mmr)
            w.run("vector", lambda e, rb=rb: e.reduce_max(out=rscr[:, 0, 0:1], in_=rb[:, 0:8],
                                                          axis=AX.X, negate=True), "vc")
            vnm = w.cnt["vc"]
            w.wait("scalar", "vc", vnm)
            sex = w.run("scalar", lambda e, rb=rb: e.activation(out=rscr[:, 1, :], in_=rb[:, 0:8],
                                                                func=AF.Exp, bias=rscr[:, 0, 0:1],
                                                                scale=1.0), "sc")
            w.wait("vector", "sc", sex)
            w.run("vector", lambda e: e.reduce_sum(out=rscr[:, 0, 1:2], in_=rscr[:, 1, :], axis=AX.X), "vc")
            w.run("vector", lambda e: e.drain(), None)
            w.run("vector", lambda e: e.reciprocal(out=rscr[:, 0, 2:3], in_=rscr[:, 0, 1:2]), "vc")
            w.run("vector", lambda e: e.drain(), None)
            w.run("vector", lambda e: e.tensor_scalar(out=rscr[:, 1, :], in0=rscr[:, 1, :],
                                                      scalar1=rscr[:, 0, 2:3], scalar2=None,
                                                      op0=ALU.mult), "vc")
            w.run("vector", lambda e: e.drain(), None)
            w.run("vector", lambda e: e.reduce_max(out=rscr[:, 0, 3:4], in_=rscr[:, 1, :], axis=AX.X), "vc")
            w.run("vector", lambda e: e.drain(), None)
            w.run("vector", lambda e: e.tensor_scalar(out=rscr[:, 2, :], in0=rscr[:, 1, :],
                                                      scalar1=rscr[:, 0, 3:4], scalar2=None,
                                                      op0=ALU.is_ge), "vc")
            w.run("vector", lambda e: e.drain(), None)
            w.run("vector", lambda e: e.scalar_tensor_tensor(out=rscr[:, 2, :], in0=rscr[:, 2, :],
                                                             scalar=-2.0, in1=rscr[:, 1, :],
                                                             op0=ALU.mult, op1=ALU.add), "vc")
            w.run("vector", lambda e: e.drain(), None)
            w.run("vector", lambda e: e.reduce_max(out=rscr[:, 0, 4:5], in_=rscr[:, 2, :], axis=AX.X), "vc")
            w.run("vector", lambda e: e.drain(), None)
            w.run("vector", lambda e: e.tensor_scalar(out=rscr[:, 2, :], in0=rscr[:, 1, :],
                                                      scalar1=rscr[:, 0, 4:5], scalar2=None,
                                                      op0=ALU.is_ge), "vc")
            w.run("vector", lambda e: e.tensor_add(out=rscr[:, 0, 5:6], in0=rscr[:, 0, 3:4],
                                                   in1=rscr[:, 0, 4:5]), "vc")
            w.run("vector", lambda e: e.drain(), None)
            w.run("vector", lambda e: e.tensor_mul(out=rscr[:, 3, :], in0=rscr[:, 1, :],
                                                   in1=rscr[:, 2, :]), "vc")
            w.run("vector", lambda e: e.reciprocal(out=rscr[:, 0, 6:7], in_=rscr[:, 0, 5:6]), "vc")
            w.run("vector", lambda e: e.drain(), None)
            aff_vc[t] = w.run("vector", lambda e, t=t, aff_h=aff_h: e.tensor_scalar(
                out=aff_h[:, t, :], in0=rscr[:, 3, :], scalar1=rscr[:, 0, 6:7],
                scalar2=None, op0=ALU.mult), "vc")
            w.release_bank(4 + t % 2, "vc", aff_vc[t])
        if debug:
            w.wait("gpsimd", "vc", aff_vc[7])
            w.run("gpsimd", lambda e, half=half, aff_h=aff_h: e.dma_start(
                out=dbg["aff"][half], in_=aff_h), "dst", 16)
        return acc_h, xh2f_h, aff_h, lda

    def c_experts(half, acc_h, xh2f_h, aff_h, lda):
        gu_lds = {(half, 0, 0): gu_load(half, 0, 0), (half, 0, 1): gu_load(half, 0, 1)}
        wd_lds = {(half, 0): wd_load(half, 0)}
        for e_ in range(c_nexp):
            for ic in range(4):
                ld_cnt = gu_lds.pop((half, e_, ic), None)
                if ld_cnt is None:
                    ld_cnt = gu_load(half, e_, ic)
                w.wait("tensor", "wd", ld_cnt)
                slot = ic % 2
                for isub in range(4):
                    for tt in range(2):
                        j = (isub * 2 + tt) % 2
                        gb, ub = ps[0 + j], ps[2 + j]
                        w.use_bank(0 + j)
                        for hc in range(8):
                            mmg = w.run("tensor", lambda e, hc=hc, isub=isub, tt=tt, slot=slot, gb=gb,
                                        xh2f_h=xh2f_h: e.matmul(
                                gb, wgr[:, slot, hc, ts_(isub, 128)], xh2f_h[:, hc, ts_(tt, 512)],
                                start=(hc == 0), stop=(hc == 7)), "tc" if hc == 7 else None)
                        w.use_bank(2 + j)
                        for hc in range(8):
                            mmu = w.run("tensor", lambda e, hc=hc, isub=isub, tt=tt, slot=slot, ub=ub,
                                        xh2f_h=xh2f_h: e.matmul(
                                ub, wur[:, slot, hc, ts_(isub, 128)], xh2f_h[:, hc, ts_(tt, 512)],
                                start=(hc == 0), stop=(hc == 7)), "tc" if hc == 7 else None)
                        w.wait("scalar", "tc", mmg)
                        w.wait("scalar", "vc", gsil_gate.get(0, 0))
                        sg = w.run("scalar", lambda e, gb=gb: e.activation(
                            out=gsil[:, 0, :], in_=gb, func=AF.Silu, bias=0.0, scale=1.0), "sc")
                        w.release_bank(0 + j, "sc", sg)
                        w.wait("vector", "tc", mmu)
                        w.wait("vector", "sc", sg)
                        vact = w.run("vector", lambda e, ub=ub, ic=ic, isub=isub, tt=tt: e.tensor_mul(
                            out=act[:, ic * 4 + isub, ts_(tt, 512)], in0=ub, in1=gsil[:, 0, :]), "vc")
                        w.release_bank(2 + j, "vc", vact)
                        gsil_gate[0] = vact
                gu_mm_done[(half, e_, ic)] = w.cnt["tc"]
                nxt = (e_, ic + 2) if ic + 2 < 4 else (e_ + 1, ic - 2)
                if nxt[0] < NE and (half,) + nxt not in gu_lds:
                    gu_lds[(half,) + nxt] = gu_load(half, *nxt)
            act_vc = w.cnt["vc"]
            w.wait("tensor", "wd", wd_lds[(half, e_)])
            w.wait("tensor", "vc", act_vc)
            for t in range(8):
                for oh in range(2):
                    bid = 4 + (t * 2 + oh) % 4
                    bank = ps[bid]
                    w.use_bank(bid)
                    for icc in range(16):
                        mmd = w.run("tensor", lambda e, icc=icc, t=t, oh=oh, bank=bank: e.matmul(
                            bank, act[:, icc, ts_(t, 128)], wdr[:, icc, ts_(oh, 512)],
                            start=(icc == 0), stop=(icc == 15)), "tc" if icc == 15 else None)
                    w.wait("vector", "tc", mmd)
                    if e_ == 0:
                        w.wait("vector", "ld", lda)
                    vd = w.run("vector", lambda e, bank=bank, t=t, oh=oh, e_=e_, acc_h=acc_h,
                               aff_h=aff_h: e.scalar_tensor_tensor(
                        out=acc_h[:, t, ts_(oh, 512)], in0=bank, scalar=aff_h[:, t, e_: e_ + 1],
                        in1=acc_h[:, t, ts_(oh, 512)], op0=ALU.mult, op1=ALU.add), "vc")
                    w.release_bank(bid, "vc", vd)
            dn_mm_done[(half, e_)] = w.cnt["tc"]
            if e_ + 1 < NE and (half, e_ + 1) not in wd_lds:
                wd_lds[(half, e_ + 1)] = wd_load(half, e_ + 1)
        acc_free_vc = w.cnt["vc"]
        w.wait("sync", "vc", acc_free_vc)
        dsto = out[ts_(half, 1024), :].rearrange("(t p) h -> p t h", p=128)
        out_st.append(w.run("sync", lambda e, dsto=dsto, acc_h=acc_h: e.dma_start(
            out=dsto, in_=acc_h), "st", 16))

    p0 = c_prologue(0)
    p1 = c_prologue(1)
    c_experts(0, *p0)
    c_experts(1, *p1)

    if debug:
        w.wait("gpsimd", "vc", w.cnt["vc"])
        w.wait("gpsimd", "tr", w.cnt["tr"])
        for icc in range(16):
            w.run("gpsimd", lambda e, icc=icc: e.dma_start(out=dbg["act"][icc], in_=act[:, icc, :]), "dst", 16)
        for hc in range(8):
            w.run("gpsimd", lambda e, hc=hc: e.dma_start(out=dbg["xh2f"][hc], in_=xh2fT[:, 0, hc, :]), "dst", 16)

    if enddbg:
        w.wait("gpsimd", "st", w.cnt["st"])
        for hc in range(8):
            w.run("gpsimd", lambda e, hc=hc: e.dma_start(out=dbg["xh2f0"][hc], in_=xh2fT[:, 0, hc, :]), "dst", 16)
        w.run("gpsimd", lambda e: e.dma_start(out=dbg["aff0"], in_=affT[:, 0]), "dst", 16)
        w.wait("gpsimd", "dst", w.cnt["dst"])

    w.wait("sync", "st", w.cnt["st"])
    if debug:
        w.wait("gpsimd", "dst", w.cnt["dst"])
    return _replay(nc, w)


def _replay(nc, w):
    with nc.Block() as block:
        @block.sync
        def _(eng):
            for f in w.thunks["sync"]:
                f(eng)

        @block.gpsimd
        def _(eng):
            for f in w.thunks["gpsimd"]:
                f(eng)

        @block.vector
        def _(eng):
            for f in w.thunks["vector"]:
                f(eng)

        @block.scalar
        def _(eng):
            for f in w.thunks["scalar"]:
                f(eng)

        @block.tensor
        def _(eng):
            for f in w.thunks["tensor"]:
                f(eng)

    return nc


def make_masks(tiles):
    m = np.zeros((NQT, 8, 128, QTW), np.float32)
    for i, g in enumerate(tiles):
        nck = NCK[i]
        q = g * QTW + np.arange(QTW)[None, :]
        for j in range(8):
            kchunk = nck - 8 + j
            k = kchunk * 128 + np.arange(128)[:, None]
            m[i, j] = (k <= q).astype(np.float32)
    return m.astype(ml_dtypes.bfloat16)


def shard_inputs(inputs):
    x = np.ascontiguousarray(inputs["L0_attn_x"])
    maps = []
    for c in range(8):
        s, h = c // 2, c % 2
        tiles = TILESETS[h]
        x_seq = np.ascontiguousarray(np.asarray(x[s], np.float32))
        x_own = np.ascontiguousarray(
            np.concatenate([x_seq[ts_(g, QTW)] for g in tiles], axis=0))
        maps.append({
            "x_seq": x_seq, "x_own": x_own, "m_in": make_masks(tiles),
            "wq_in": inputs["L0_attn_Wq"], "wk_in": inputs["L0_attn_Wk"],
            "wv_in": inputs["L0_attn_Wv"], "wo_in": inputs["L0_attn_Wo"],
            "bq_in": inputs["L0_attn_bq"], "bk_in": inputs["L0_attn_bk"],
            "bv_in": inputs["L0_attn_bv"], "bo_in": inputs["L0_attn_bo"],
            "rw_in": inputs["L0_ffn_router_w"], "wg_in": inputs["L0_ffn_w_gate"],
            "wu_in": inputs["L0_ffn_w_up"], "wd_in": inputs["L0_ffn_w_down"],
        })
    return maps


def kernel(**inputs):
    inputs = {k: np.asarray(v) for k, v in inputs.items()}
    maps = shard_inputs(inputs)
    nc = build_nc(debug=False)
    res = run_bass_kernel_spmd(nc, maps, core_ids=list(range(8)))
    outf = np.zeros((4, SEQ, D), np.float32)
    for c in range(8):
        s, h = c // 2, c % 2
        o = res.results[c]["out"]
        for i, g in enumerate(TILESETS[h]):
            outf[s, ts_(g, QTW)] = o[ts_(i, QTW)]
    return outf
